# version 11
# baseline (speedup 1.0000x reference)
import numpy as np
import ml_dtypes
from contextlib import ExitStack

import concourse.bass as bass
import concourse.mybir as mybir
from concourse.bass_utils import run_bass_kernel_spmd

F32, BF16 = mybir.dt.float32, mybir.dt.bfloat16
AF, ALU = mybir.ActivationFunctionType, mybir.AluOpType
AX = mybir.AxisListType.X
BFNP = ml_dtypes.bfloat16

H, D, BS, DM = 32, 64, 256, 2048
T, TH, NB = 1024, 512, 4
NCH = 16

ENGS = ("tensor", "vector", "scalar", "gpsimd", "sync")
SEMS = ["sx", "sw", "gd", "t0", "t1", "r0", "r1", "pk0", "pk1", "q0", "q1",
        "d0", "d1", "vl0", "vl1", "ms", "mq", "o0", "o1", "wt",
        "mm", "vv", "sc", "gp", "cc", "bar", "wt2", "pg0", "pg1",
        "cs0", "cw0"]


def host_constants():
    base = np.float32(1.0) / np.float32(2.0) ** np.float32(8.0 / H)
    rate = base ** (np.arange(H, dtype=np.float32) + 1.0)
    factor = np.float32(1.0 - 0.0 / (32 - 1 + 1e-05) + 1e-05)
    slope = (rate * factor).astype(np.float32)
    r = np.arange(BS, dtype=np.float32) + 1.0
    qd = np.exp(-slope[:, None] * r[None, :]).astype(np.float32)
    kd = np.exp(-slope[:, None] * (BS - r[None, :])).astype(np.float32)
    arg = slope[:, None, None] * (r[:, None] - r[None, :])[None]
    dd = np.where(arg >= 0, np.exp(-np.abs(arg)), 0.0).astype(np.float32)
    bd = np.exp(-slope * np.float32(BS)).astype(np.float32)

    ddt = np.zeros((H, 128, 384), np.float32)
    for h in range(H):
        dT = dd[h].T
        ddt[h, :, 0:256] = dT[0:128, :]
        ddt[h, :, 256:384] = dT[128:256, 128:256]
    qdt = np.zeros((NCH, 128, 256), np.float32)
    for c in range(NCH):
        for p in range(128):
            qdt[c, p, :] = qd[2 * c + p // 64]
    kdc = np.zeros((128, 64), np.float32)
    for h in range(H):
        for jt in range(2):
            kdc[:, 2 * h + jt] = kd[h, 128 * jt:128 * jt + 128]
    bdr = np.tile(np.repeat(bd, D)[None, :], (128, 1)).astype(np.float32)
    psrs = []
    for c in range(8):
        psm = np.zeros((8, 2048), np.float32)
        g0 = (c // 4) * 4
        for j in range(g0, c):
            psm[j] = np.repeat(bd ** (4 * (c - 1 - j)), D)
        psrs.append(psm)
    return ddt.astype(BFNP), qdt.astype(BFNP), kdc, \
        bdr.astype(BFNP), psrs


class Prog:
    def __init__(self):
        self.streams = {k: [] for k in ENGS}
        self.cnt = {}
        self.sems = None
        self.done = False

    def raw(self, eng, fn):
        if self.done:
            return
        self.streams[eng].append(fn)

    def tag(self, eng, fn, sem, n=1):
        if self.done:
            return self.cnt.get(sem, 0)
        self.cnt[sem] = self.cnt.get(sem, 0) + n
        tgt = self.cnt[sem]
        P = self

        def wrapped(e, fn=fn, sem=sem, n=n):
            fn(e).then_inc(P.sems[sem], n)
        self.streams[eng].append(wrapped)
        return tgt

    def wait(self, eng, sem, val):
        if self.done or val is None or val <= 0:
            return
        P = self

        def w(e, sem=sem, val=val):
            e.wait_ge(P.sems[sem], val)
        self.streams[eng].append(w)

    def now(self, sem):
        return self.cnt.get(sem, 0)

    def barrier(self):
        base = self.cnt.get("bar", 0)
        n = len(ENGS)
        for eng in ENGS:
            self.tag(eng, lambda e: e.drain(), "bar")
        for eng in ENGS:
            self.wait(eng, "bar", base + n)


def build_program(stage=None):
    nc = bass.Bass("TRN2", target_bir_lowering=False,
                   detect_race_conditions=False)
    dt = nc.dram_tensor
    xT = dt("xT", [DM, T], F32, kind="ExternalInput").ap()
    wq = dt("wq", [8, 128, 4096], F32, kind="ExternalInput").ap()
    wk = dt("wk", [8, 128, 4096], F32, kind="ExternalInput").ap()
    wv = dt("wv", [8, 128, 4096], F32, kind="ExternalInput").ap()
    wg = dt("wg", [8, 128, 4096], F32, kind="ExternalInput").ap()
    wo = dt("wo", [8, 128, 4096], F32, kind="ExternalInput").ap()
    wf = dt("wf", [1, 128, 4096], F32, kind="ExternalInput").ap()
    biasr = dt("biasr", [128, NCH], F32, kind="ExternalInput").ap()
    nwi = dt("nwi", [128, NCH], F32, kind="ExternalInput").ap()
    ddi = dt("ddi", [H, 128, 384], BF16, kind="ExternalInput").ap()
    qdi = dt("qdi", [NCH, 128, 256], BF16, kind="ExternalInput").ap()
    kdi = dt("kdi", [128, 64], F32, kind="ExternalInput").ap()
    bdi = dt("bdi", [128, DM], BF16, kind="ExternalInput").ap()
    psi = dt("psi", [8, DM], F32, kind="ExternalInput").ap()
    idi = dt("idi", [128, 128], BF16, kind="ExternalInput").ap()
    yT = dt("yT", [DM, T], F32, kind="ExternalOutput").ap()
    x1p = dt("x1p", [DM, T], F32).ap()
    vp = dt("vp", [8, 8, 128, 256], BF16).ap()
    dsc = dt("dsc", [128, 1], F32).ap()
    cci = dt("cci", [64, DM], F32)
    cco = dt("cco", [512, DM], F32)
    ccs_i = dt("ccs_i", [1, 1], F32)
    ccs_o = dt("ccs_o", [1, 1], F32)
    ccw_i = dt("ccw_i", [128, 4096], BF16)
    wtg_t = dt("wtg", [1024, 4096], BF16)
    wtg = wtg_t.ap()
    cw_hold = []

    P = Prog()
    st = ExitStack()
    e = st.enter_context
    xa_f = e(nc.sbuf_tensor("xa", [128, 8192], F32))
    xa_b = xa_f.ap().bitcast(BF16)
    kva_t = e(nc.sbuf_tensor("kva", [128, 16384], BF16))
    kva = kva_t.ap()
    kva_f = kva.bitcast(F32)
    ga_t = e(nc.sbuf_tensor("ga", [128, 16384], BF16))
    ga = ga_t.ap()
    xbf_t = e(nc.sbuf_tensor("xbf", [128, 16384], BF16))
    xbf = xbf_t.ap()
    xbf_f = xbf.bitcast(F32)
    va = e(nc.sbuf_tensor("va", [128, 6144], BF16))
    ww = e(nc.sbuf_tensor("ww", [128, 4096], BF16))
    ss_t = e(nc.sbuf_tensor("ss", [128, 3072], F32))
    ss = ss_t.ap()
    ss_b = ss.bitcast(BF16)
    s2t = e(nc.sbuf_tensor("s2t", [128, 4096], BF16))
    cs = e(nc.sbuf_tensor("cs", [128, 4096], BF16))
    sm = e(nc.sbuf_tensor("sm", [128, DM], F32))
    bdr_s = e(nc.sbuf_tensor("bdr_s", [128, DM], BF16))
    qdr = e(nc.sbuf_tensor("qdr", [128, 4096], BF16))
    kdc_s = e(nc.sbuf_tensor("kdc_s", [128, 64], F32))
    idb = e(nc.sbuf_tensor("idb", [128, 128], BF16))
    nwc = e(nc.sbuf_tensor("nwc", [128, NCH], F32))
    acc = e(nc.sbuf_tensor("acc", [128, 2], F32))
    thrc = e(nc.sbuf_tensor("thrc", [128, 1], F32))
    onescf = e(nc.sbuf_tensor("onescf", [128, 1], F32))
    onescb = e(nc.sbuf_tensor("onescb", [128, 1], BF16))
    onesrf = e(nc.sbuf_tensor("onesrf", [1, 128], F32))
    onesrb = e(nc.sbuf_tensor("onesrb", [1, 512], BF16))
    b05c = e(nc.sbuf_tensor("b05c", [128, NCH], F32))
    arow = e(nc.sbuf_tensor("arow", [1, 1024], F32))
    brow = e(nc.sbuf_tensor("brow", [1, 1024], F32))
    s05v = e(nc.sbuf_tensor("s05v", [1, 1], F32))
    s05c = e(nc.sbuf_tensor("s05c", [128, 1], F32))
    ps_t = e(nc.psum_tensor("ps", [128, 4096], F32))
    ps = ps_t.ap()
    psb = ps.bitcast(BF16)
    sems = {n: e(nc.semaphore(n)) for n in SEMS}
    P.sems = sems

    def bank(i, w=512):
        return ps[:, 512 * i: 512 * i + w]

    def checkpoint(name, dump_ap=None, cast=False):
        if stage == name:
            P.barrier()
            if dump_ap is not None:
                if cast:
                    raw = dump_ap.bitcast(F32) if dump_ap.dtype == BF16 else dump_ap
                    n = dma("sync",
                            yT[0:1024, :].rearrange("(a p) t -> p a t", p=128),
                            raw.rearrange("p (a t) -> p a t", a=8), "o0")
                    n = dma("scalar",
                            yT[1024:2048, :].rearrange("(a p) t -> p a t",
                                                       p=128),
                            raw.rearrange("p (a t) -> p a t", a=8), "o0")
                else:
                    n = dma("sync", yT.rearrange("(a p) t -> p a t", p=128),
                            dump_ap, "o0")
                for eng in ENGS:
                    P.wait(eng, "o0", n)
            P.done = True

    def dma(q, out, in_, sem):
        return P.tag(q, lambda e, o=out, i=in_: e.dma_start(out=o, in_=i),
                     sem, 16)

    def mm(out, lhsT, rhs, start, stop, tp=None):
        return P.tag("tensor",
                     lambda e, o=out, l=lhsT, r=rhs, s=start, p=stop, t=tp:
                     e.matmul(o, l, r, start=s, stop=p, tile_position=t),
                     "mm")

    def vv(fn):
        return P.tag("vector", fn, "vv")

    def vvw(fn):
        t = P.tag("vector", fn, "vv")
        P.wait("vector", "vv", t)
        return t

    def sc(fn):
        return P.tag("scalar", fn, "sc")

    def scw(fn):
        t = P.tag("scalar", fn, "sc")
        P.wait("scalar", "sc", t)
        return t

    def xav(c):
        return xa_f[:, 512 * c: 512 * c + 512]

    def qv(c, t0, n):
        return xa_b[:, 1024 * c + t0: 1024 * c + t0 + n]

    def kv(c, t0, n):
        return kva[:, 1024 * c + t0: 1024 * c + t0 + n]

    def gav(c, t0, n):
        return ga[:, 1024 * c + t0: 1024 * c + t0 + n]

    def xbfv(c, t0, n):
        return xbf[:, 1024 * c + t0: 1024 * c + t0 + n]

    def attnv(h, c, i0, n):
        return xbf[:, 8192 * h + 512 * c + i0: 8192 * h + 512 * c + i0 + n]

    WBUFS = None

    def wwv(w, ic):
        buf = ww if w % 2 == 0 else qdr.ap()
        return buf[:, 256 * ic: 256 * ic + 256]

    dma("sync", bdr_s.ap(), bdi, "sx")
    dma("sync", kdc_s.ap(), kdi, "sx")
    dma("sync", idb.ap(), idi, "sx")
    dma("sync", nwc.ap(), nwi, "sx")
    c_s0 = dma("sync", ss[:, 2080:2080 + NCH], biasr, "ms")
    P.raw("gpsimd", lambda e: e.memset(onescf.ap(), 1.0))
    P.raw("gpsimd", lambda e: e.memset(onescb.ap(), 1.0))
    P.raw("gpsimd", lambda e: e.memset(onesrf.ap(), 1.0))
    P.raw("gpsimd", lambda e: e.memset(onesrb.ap(), 1.0))
    P.tag("gpsimd", lambda e: e.memset(sm.ap(), 0.0), "gp")
    P.wait("vector", "ms", c_s0)
    P.wait("vector", "gp", P.now("gp"))
    vv_b05 = vvw(lambda e: e.tensor_scalar(
        b05c.ap(), ss[:, 2080:2080 + NCH], 0.5, None, ALU.mult))
    v_ms0 = vv(lambda e: e.memset(acc.ap(), 0.0))
    P.wait("vector", "vv", v_ms0)

    kva_f2 = kva.bitcast(F32)
    SLAB = kva_f2[:, 0:4096]
    va_f = va.ap().bitcast(F32)
    d = dma("gpsimd", SLAB, wf[0], "t0")
    P.wait("vector", "t0", d)
    v_r = vv(lambda e: e.reduce_sum(acc[:, 0:1], SLAB, axis=AX,
                                    apply_absolute_value=True))
    P.wait("sync", "vv", v_r)
    d_a1 = dma("sync", dsc, acc[:, 0:1], "ms")
    P.wait("sync", "ms", d_a1)
    d_ac = dma("sync", va_f[0:1, 0:128], dsc.rearrange("p o -> o p"), "ms")
    P.wait("vector", "ms", d_ac)
    v_p = vvw(lambda e: e.reduce_sum(va_f[0:1, 128:129], va_f[0:1, 0:128],
                                     axis=AX))
    P.wait("gpsimd", "vv", v_p)
    d_ci0 = dma("gpsimd", ccs_i.ap(), va_f[0:1, 128:129], "gd")
    P.wait("gpsimd", "gd", d_ci0)
    cs0_t = P.tag("gpsimd", lambda e: e.collective_compute(
        "AllReduce", ALU.add, replica_groups=[list(range(8))],
        ins=[ccs_i.ap()], outs=[ccs_o.ap()]), "cs0")
    P.wait("tensor", "gp", P.now("gp"))

    gaf = ga.bitcast(F32)

    def ln_stats_apply(src_view, dst_view):
        cast_sl = [ss_b[:, 4096 + 512 * i:4608 + 512 * i] for i in range(4)]
        sq_sl = [cs[:, 512 * i:512 * i + 512] for i in range(4)]
        sumc = {0: 3072, 1: 2048}
        sqc = {0: 3584, 1: 2560}
        mm_ch = {}
        for i in range(2 * NCH):
            h, c = i // NCH, i % NCH
            if i >= 4:
                P.wait("scalar", "mm", mm_ch[i - 4])
            s_c = sc(lambda e, h=h, c=c, s=cast_sl[i % 4]:
                     e.copy(s, src_view(h, c)))
            s_q = sc(lambda e, s=cast_sl[i % 4], q=sq_sl[i % 4]:
                     e.activation(q, s, AF.Square))
            P.wait("tensor", "sc", s_c)
            mm(ps[0:1, sumc[h]:sumc[h] + 512], onescb[:, 0:1],
               cast_sl[i % 4], c == 0, c == NCH - 1)
            P.wait("tensor", "sc", s_q)
            mm(ps[0:1, sqc[h]:sqc[h] + 512], onescb[:, 0:1],
               sq_sl[i % 4], c == 0, c == NCH - 1)
            mm_ch[i] = P.now("mm")
        P.wait("vector", "mm", P.now("mm"))
        mrow = ss[0:1, 1024:2048]
        qrow = ss[0:1, 2048:3072]
        trow = ss[0:1, 0:1024]
        vvw(lambda e: e.tensor_copy(ss[0:1, 1024:1536], ps[0:1, 3072:3584]))
        vvw(lambda e: e.tensor_copy(ss[0:1, 1536:2048], ps[0:1, 2048:2560]))
        vvw(lambda e: e.tensor_copy(ss[0:1, 2048:2560], ps[0:1, 3584:4096]))
        vvw(lambda e: e.tensor_copy(ss[0:1, 2560:3072], ps[0:1, 2560:3072]))
        vvw(lambda e: e.tensor_scalar(mrow, mrow, 1.0 / 2048.0, None,
                                      ALU.mult))
        vvw(lambda e: e.tensor_scalar(qrow, qrow, 1.0 / 2048.0, None,
                                      ALU.mult))
        vvw(lambda e: e.tensor_tensor(trow, mrow, mrow, ALU.mult))
        vvw(lambda e: e.scalar_tensor_tensor(
            trow, trow, -1.0, qrow, ALU.mult, ALU.add))
        v_var = vvw(lambda e: e.tensor_scalar(trow, trow, 1e-5, None,
                                              ALU.add))
        P.wait("scalar", "vv", v_var)
        s_sq = scw(lambda e: e.activation(qrow, trow, AF.Sqrt))
        P.wait("vector", "sc", s_sq)
        vvw(lambda e: e.reciprocal(arow[0:1, 0:1024], qrow))
        v_ab = vvw(lambda e: e.scalar_tensor_tensor(
            brow[0:1, 0:1024], mrow, -1.0, arow[0:1, 0:1024],
            ALU.mult, ALU.mult))
        P.wait("tensor", "vv", v_ab)
        AB = {0: (4, 5), 1: (2, 3)}
        mm(bank(4), onesrf[0:1, :], arow[0:1, 0:512], True, True)
        mm(bank(5), onesrf[0:1, :], brow[0:1, 0:512], True, True)
        mm(bank(2), onesrf[0:1, :], arow[0:1, 512:1024], True, True)
        mm(bank(3), onesrf[0:1, :], brow[0:1, 512:1024], True, True)
        P.wait("vector", "mm", P.now("mm"))
        tmp_sl = [ss[:, 0:512], ss[:, 512:1024]]
        for i in range(2 * NCH):
            h, c = i // NCH, i % NCH
            vvw(lambda e, h=h, c=c, t=tmp_sl[i % 2], a=AB[h][0]:
                e.tensor_tensor(t, src_view(h, c), bank(a), ALU.mult))
            vv(lambda e, h=h, c=c, t=tmp_sl[i % 2], b=AB[h][1]:
               e.tensor_tensor(dst_view(h, c), t, bank(b), ALU.add))
        return P.now("vv")

    checkpoint("s0")
    def h1v(c):
        base = gaf if c < 8 else kva_f2
        cc = c % 8
        return base[:, 4096 + 512 * cc:4608 + 512 * cc]

    def ln_src(h, c):
        return xav(c) if h == 0 else h1v(c)

    for c in range(NCH):
        dma("sync", xav(c), xT[128 * c:128 * c + 128, 0:TH], "sx")
        dma("scalar", h1v(c), xT[128 * c:128 * c + 128, TH:T], "sx")
    P.wait("scalar", "sx", P.now("sx"))
    P.wait("vector", "sx", P.now("sx"))
    ln_stats_apply(ln_src, lambda h, c: xbfv(c, TH * h, TH))
    P.barrier()
    checkpoint("s1h0", xbf, cast=True)
    P.wait("sync", "cs0", cs0_t)
    d_so = dma("sync", va_f[0:1, 128:129], ccs_o.ap(), "ms")
    P.wait("vector", "ms", d_so)
    vvw(lambda e: e.tensor_scalar(
        s05v[0:1, 0:1], va_f[0:1, 128:129], 0.5 / 4194304.0, None, ALU.mult))
    v_t0 = vvw(lambda e: e.tensor_scalar(
        va_f[0:1, 129:130], va_f[0:1, 128:129], 0.5 / 4194304.0, 5e-11,
        ALU.mult, ALU.add))
    v_t0b = vvw(lambda e: e.tensor_copy(thrc[0:1, 0:1], va_f[0:1, 129:130]))
    v_t0c = vvw(lambda e: e.tensor_copy(s05c[0:1, 0:1], s05v[0:1, 0:1]))
    P.wait("sync", "vv", v_t0c)
    k = 1
    while k < 128:
        dma("sync", thrc[k:2 * k, 0:1], thrc[0:k, 0:1], "ms")
        dma("sync", s05c[k:2 * k, 0:1], s05c[0:k, 0:1], "ms")
        P.wait("sync", "ms", P.now("ms"))
        k *= 2
    P.wait("vector", "ms", P.now("ms"))
    vvw(lambda e: e.tensor_scalar(acc[:, 1:2], thrc[:, 0:1], -1.0, None,
                                  ALU.mult))
    vv(lambda e: e.tensor_scalar(
        ss_b[:, 0:4096], SLAB, thrc[:, 0:1], s05c[:, 0:1],
        ALU.is_gt, ALU.mult))
    vvw(lambda e: e.tensor_scalar(
        ga[:, 0:4096], SLAB, acc[:, 1:2], s05c[:, 0:1],
        ALU.is_lt, ALU.mult))
    tn = vv(lambda e: e.tensor_sub(
        ga[:, 4096:8192], ss_b[:, 0:4096], ga[:, 0:4096]))
    P.wait("gpsimd", "vv", tn)
    d_cw = dma("gpsimd", ccw_i.ap(), ga[:, 4096:8192], "wt")
    P.wait("gpsimd", "wt", d_cw)
    cw_hold.append(P.tag("gpsimd", lambda e: e.collective_compute(
        "AllGather", ALU.bypass, replica_groups=[list(range(8))],
        ins=[ccw_i.ap()], outs=[wtg_t.ap()]), "cw0"))

    def win_phase(fill, rhs_view, n_ic, evac, bias_on=False, pre=None):
        guards = {}
        mcnt = {}
        for w in range(8):
            rsem, rval = fill(w, mcnt.get(w - 2, 0))
            if pre is not None:
                pre(w)
            P.wait("tensor", rsem, rval)
            if w >= 2:
                gsem, gval = guards[w - 2]
                P.wait("tensor", gsem, gval)
            bset = 4 * (w % 2)
            for ic in range(n_ic):
                for oc in range(2):
                    for th in range(2):
                        mm(bank(bset + 2 * oc + th),
                           wwv(w, ic)[:, 128 * oc:128 * oc + 128],
                           rhs_view(ic, th), ic == 0, ic == n_ic - 1)
            mcnt[w] = P.now("mm")
            guards[w] = evac(w, bset, mcnt[w])
        for w in (6, 7):
            gsem, gval = guards[w]
            P.wait("tensor", gsem, gval)
        return mcnt

    checkpoint("s1", xbf, cast=True)
    x0sl = [ss[:, 512 * i:512 * i + 512] for i in range(4)]
    x1sl = [ss[:, 2048:2560], ss[:, 2560:3072]]

    def wt_fill_q(q, sem):
        def fill(w, prev_mm):
            P.wait(q, "mm", prev_mm)
            if w == 0:
                P.wait(q, "cw0", cw_hold[0])
            buf = ww if w % 2 == 0 else qdr.ap()
            d = dma(q, buf[:, 0:4096], wtg[128 * w:128 * w + 128, :], sem)
            return (sem, d)
        return fill

    def ffn_pre_evac(res_src, out_dst):
        RS = ["r0", "r1"]
        PK = ["pg0", "pg1"]
        rv_of = {}

        def pre(w):
            P.wait("sync", "vv", rv_of.get(("tt", w - 1), 0))
            for oc in range(2):
                for th in range(2):
                    oca = 2 * w + oc
                    rv_of[(w, oc, th)] = dma(
                        "sync", x0sl[2 * oc + th],
                        res_src[128 * oca:128 * oca + 128,
                                512 * th:512 * th + 512], RS[w % 2])

        def evac(w, bset, m_w):
            P.wait("vector", "mm", m_w)
            P.wait("vector", RS[w % 2], rv_of[(w, 1, 1)])
            last = 0
            for oc in range(2):
                for th in range(2):
                    oca = 2 * w + oc
                    sl = th
                    P.wait("vector", PK[sl], P.now(PK[sl]))
                    last = vv(lambda e, b=bset + 2 * oc + th, oca=oca,
                              s=x0sl[2 * oc + th], o=x1sl[sl]:
                              e.scalar_tensor_tensor(
                                  o, ps[:, 512 * b:512 * b + 512],
                                  b05c[:, oca:oca + 1], s, ALU.add,
                                  ALU.add))
                    P.wait("gpsimd", "vv", last)
                    dma("gpsimd",
                        out_dst[128 * oca:128 * oca + 128,
                                512 * th:512 * th + 512], x1sl[sl], PK[sl])
            rv_of[("tt", w)] = last
            return ("vv", last)
        return pre, evac

    f1pre, f1evac = ffn_pre_evac(xT, x1p)
    P.wait("sync", "vv", tn)
    win_phase(wt_fill_q("scalar", "sw"), lambda ic, th: xbfv(ic, 512 * th, 512),
              NCH, f1evac, bias_on=True, pre=f1pre)
    P.barrier()

    checkpoint("s2")
    for q in ("sync", "scalar"):
        P.wait(q, "pg0", P.now("pg0"))
        P.wait(q, "pg1", P.now("pg1"))
    for c in range(NCH):
        dma("sync", xav(c), x1p[128 * c:128 * c + 128, 0:TH], "sx")
        dma("scalar", h1v(c), x1p[128 * c:128 * c + 128, TH:T], "sx")
    P.wait("scalar", "sx", P.now("sx"))
    P.wait("vector", "sx", P.now("sx"))
    ln_stats_apply(ln_src, lambda h, c: xbfv(c, TH * h, TH))
    P.barrier()

    checkpoint("s3", xbf, cast=True)
    def cast_fill(wsrc):
        def fill(w, prev_mm):
            P.wait("gpsimd", "mm", prev_mm)
            buf = ww if w % 2 == 0 else qdr.ap()
            d = dma("gpsimd", buf[:, 0:4096], wsrc[w], "gd")
            return ("gd", d)
        return fill

    def silu_evac(dst_view, act):
        def evac(w, bset, m_w):
            P.wait("scalar", "mm", m_w)
            last = 0
            for oc in range(2):
                for th in range(2):
                    last = sc(lambda e, b=bset + 2 * oc + th,
                              d=dst_view(2 * w + oc, th): e.activation(
                                  d, ps[:, 512 * b:512 * b + 512], act))
            return ("sc", last)
        return evac

    win_phase(cast_fill(wk), lambda ic, th: xbfv(ic, 512 * th, 512), NCH,
              silu_evac(lambda oca, th: kv(oca, 512 * th, 512), AF.Silu))
    P.barrier()

    checkpoint("s4a", kva, cast=True)
    kkd_bufs = [ss_b[:, 0:2048], ss_b[:, 2048:4096]]
    vsl = [va[:, 0:2048], va[:, 2048:4096]]
    PKS = ["pk0", "pk1"]
    ev_v = {}
    mv = {}
    pk_tot = {0: 0, 1: 0}
    for w in range(8):
        P.wait("gpsimd", "mm", mv.get(w - 2, 0))
        buf = ww if w % 2 == 0 else qdr.ap()
        d = dma("gpsimd", buf[:, 0:4096], wv[w], "gd")
        P.wait("tensor", "gd", d)
        if w >= 2:
            P.wait("tensor", "sc", ev_v[w - 2])
        bset = 4 * (w % 2)
        for tt in range(8):
            bk, off = bset + tt % 4, 256 * (tt // 4)
            for ic in range(NCH):
                mm(ps[:, 512 * bk + off:512 * bk + off + 256],
                   xbfv(ic, 128 * tt, 128), wwv(w, ic), ic == 0,
                   ic == NCH - 1)
        mv[w] = P.now("mm")
        P.wait("scalar", "mm", mv[w])
        P.wait("scalar", PKS[w % 2], pk_tot[w % 2])
        last = 0
        for tt in range(8):
            bk, off = bset + tt % 4, 256 * (tt // 4)
            scol = 256 * (8 * (w % 2) + tt)
            last = sc(lambda e, b=bk, o=off, sc0=scol: e.activation(
                va[:, sc0:sc0 + 256], ps[:, 512 * b + o:512 * b + o + 256],
                AF.Silu))
            P.wait("sync", "sc", last)
            dma("sync", vp[tt, w], va[:, scol:scol + 256], PKS[w % 2])
        ev_v[w] = last
        pk_tot[w % 2] = P.now(PKS[w % 2])
    P.barrier()

    c_started = set()
    prev_cmm = 0
    for tt in range(8):
        tau, jt = tt // 2, tt % 2
        dstv = kkd_bufs[tt % 2]
        P.wait("sync", "mm", prev_cmm)
        vld = dma("sync", vsl[tt % 2].rearrange("p (w j) -> p w j", w=8),
                  vp[tt].rearrange("w p j -> p w j"),
                  ["vl0", "vl1"][tt % 2])
        kkd_ev = {}
        for c in range(NCH):
            bk = c % 2
            tp_dst = psb[:, 1024 * bk:1024 * bk + 128]
            if c >= 2:
                P.wait("tensor", "vv", kkd_ev[c - 2])
            P.tag("tensor", lambda e, d=tp_dst, c=c, tau=tau, jt=jt:
                  e.matmul(d, kv(c, 256 * tau + 128 * jt, 128), idb.ap(),
                           start=True, stop=True, is_transpose=True), "mm")
            m = P.now("mm")
            P.wait("vector", "mm", m)
            for p in range(2):
                hd = 2 * c + p
                kkd_ev[c] = vv(lambda e, d=tp_dst, c=c, p=p, hd=hd, jt=jt,
                               dv=dstv: e.tensor_scalar(
                    dv[:, 128 * c + 64 * p:128 * c + 64 * p + 64],
                    d[:, 64 * p:64 * p + 64],
                    kdc_s[:, 2 * hd + jt:2 * hd + jt + 1], None, ALU.mult))
        kkd_done = P.now("vv")
        P.wait("tensor", "vv", kkd_done)
        P.wait("tensor", ["vl0", "vl1"][tt % 2], vld)
        rhalf = 64 * (tau // 2)
        for hd in range(32):
            cbank = 4 + (64 * hd) // 512
            key = (cbank, tau)
            first = key not in c_started
            c_started.add(key)
            last_c = (jt == 1) and (hd % 8 == 7)
            tp = (0, 64) if rhalf else None
            mm(ps[rhalf:rhalf + 64, 2048 + 64 * hd:2048 + 64 * hd + 64],
               dstv[:, 128 * (hd // 2) + 64 * (hd % 2):
                    128 * (hd // 2) + 64 * (hd % 2) + 64],
               vsl[tt % 2][:, 64 * hd:64 * hd + 64],
               first, last_c, tp=tp)
        prev_cmm = P.now("mm")
        if jt == 1:
            P.wait("vector", "mm", prev_cmm)
            cv = vv(lambda e, r=rhalf, tau=tau: e.tensor_copy(
                cs[r:r + 64, 2048 * (tau % 2):2048 * (tau % 2) + 2048],
                ps[r:r + 64, 2048:4096]))
            P.wait("tensor", "vv", cv)
    P.barrier()
    checkpoint("s4b")
    dma("sync", ss_b[0:64, 0:2048], cs[64:128, 0:2048], "ms")
    d_c3 = dma("sync", ss_b[0:64, 4096:6144], cs[64:128, 2048:4096], "ms")
    P.wait("vector", "ms", d_c3)
    Pv = ps[0:64, 0:2048]
    vvw(lambda e: e.tensor_copy(Pv, cs[0:64, 0:2048]))
    vvw(lambda e: e.tensor_tensor(Pv, Pv, bdr_s[0:64, :], ALU.mult))
    vvw(lambda e: e.tensor_add(Pv, Pv, cs[0:64, 2048:4096]))
    vvw(lambda e: e.tensor_tensor(Pv, Pv, bdr_s[0:64, :], ALU.mult))
    vvw(lambda e: e.tensor_add(Pv, Pv, ss_b[0:64, 0:2048]))
    vvw(lambda e: e.tensor_tensor(Pv, Pv, bdr_s[0:64, :], ALU.mult))
    v_pub = vvw(lambda e: e.tensor_add(ss[0:64, 0:2048], Pv,
                                       ss_b[0:64, 4096:6144]))
    P.wait("gpsimd", "vv", v_pub)
    d_ci = dma("gpsimd", cci.ap(), ss[0:64, 0:2048], "gd")
    P.wait("gpsimd", "gd", d_ci)
    cc_t = P.tag("gpsimd", lambda e: e.collective_compute(
        "AllGather", ALU.bypass, replica_groups=[list(range(8))],
        ins=[cci.ap()], outs=[cco.ap()]), "cc")
    P.barrier()

    checkpoint("s4c")
    win_phase(cast_fill(wq), lambda ic, th: xbfv(ic, 512 * th, 512), NCH,
              silu_evac(lambda oca, th: qv(oca, 512 * th, 512), AF.Silu))
    P.barrier()
    win_phase(cast_fill(wg), lambda ic, th: xbfv(ic, 512 * th, 512), NCH,
              silu_evac(lambda oca, th: gav(oca, 512 * th, 512), AF.Sigmoid))
    P.barrier()

    checkpoint("s5", xa_b, cast=True)
    P.wait("gpsimd", "cc", cc_t)
    P.barrier()
    dma("sync", qdr.ap().rearrange("p (a t) -> p a t", a=16),
        qdi.rearrange("a p t -> p a t"), "sx")
    P.wait("vector", "sx", P.now("sx"))
    va_f = va.ap().bitcast(F32)
    for j in range(7):
        dj = dma("scalar", va_f[0:1, 0:DM], psi[j:j + 1, :], "mq")
        ds = dma("sync", ss[0:64, 1024:3072],
                 cco.ap()[64 * j:64 * j + 64, :], "ms")
        P.wait("tensor", "mq", dj)
        for q4 in range(4):
            mm(ps[0:64, 2048 + 512 * q4:2048 + 512 * q4 + 512],
               onesrf[0:1, 0:64], va_f[0:1, 512 * q4:512 * q4 + 512],
               True, True)
        P.wait("vector", "mm", P.now("mm"))
        P.wait("vector", "ms", ds)
        vvw(lambda e: e.tensor_tensor(ps[0:64, 0:2048], ss[0:64, 1024:3072],
                                      ps[0:64, 2048:4096], ALU.mult))
        v_j = vvw(lambda e: e.tensor_add(sm[0:64, :], sm[0:64, :],
                                         ps[0:64, 0:2048]))
        P.wait("tensor", "vv", v_j)
        P.wait("sync", "vv", v_j)
        P.wait("scalar", "vv", v_j)
    P.barrier()

    checkpoint("s6a")
    awt_p = [[ss_b[:, 4096:4480], ss_b[:, 4480:4864]],
             [ss_b[:, 4864:5248], ss_b[:, 5248:5632]]]
    qq = ss_b[:, 0:4096]
    dd_a = [ww[:, 0:384], ww[:, 384:768]]
    dd_b = [ww[:, 768:1152], ww[:, 1152:1536]]
    sq_sl = [ss_b[:, 5632:6144], ss_b[:, 2048:2560]]

    for h in range(2):
        rh = 64 * h
        if h == 1:
            P.wait("sync", "vv", P.now("vv"))
            dsh = dma("sync", sm[64:128, :], sm[0:64, :], "ms")
            P.wait("vector", "ms", dsh)
        for k in range(2):
            tau = 2 * h + k
            slot = tau % 2
            c1 = vvw(lambda e, r=rh, s=slot: e.tensor_copy(
                s2t[r:r + 64, 2048 * s:2048 * s + 2048], sm[r:r + 64, :]))
            P.wait("sync", "vv", c1)
            oth = 64 - rh
            dma("sync", s2t[oth:oth + 64, 2048 * slot:2048 * slot + 2048],
                s2t[rh:rh + 64, 2048 * slot:2048 * slot + 2048], "ms")
            vvw(lambda e, r=rh: e.tensor_tensor(
                sm[r:r + 64, :], sm[r:r + 64, :], bdr_s[r:r + 64, :],
                ALU.mult))
            vvw(lambda e, r=rh, tau=tau: e.tensor_add(
                sm[r:r + 64, :], sm[r:r + 64, :],
                cs[r:r + 64, 2048 * (tau % 2):2048 * (tau % 2) + 2048]))
        P.wait("tensor", "ms", P.now("ms"))
        P.wait("tensor", "vv", P.now("vv"))

        for k in range(2):
            tau = 2 * h + k
            slot = tau % 2
            P.wait("sync", "mm", P.now("mm"))
            vload = {}
            for jt in range(2):
                tt = 2 * tau + jt
                vload[jt] = dma("sync", vsl[jt].rearrange(
                                    "p (w j) -> p w j", w=8),
                                vp[tt].rearrange("w p j -> p w j"),
                                ["vl0", "vl1"][jt])
            P.wait("vector", "mm", P.now("mm"))
            q_done = {}
            for c in range(NCH):
                q_done[c] = vv(lambda e, c=c, tau=tau: e.tensor_tensor(
                    qq[:, 256 * c:256 * c + 256],
                    xa_b[:, 1024 * c + 256 * tau:1024 * c + 256 * tau + 256],
                    qdr[:, 256 * c:256 * c + 256], ALU.mult))
            P.wait("tensor", "vl0", P.now("vl0"))
            P.wait("tensor", "vl1", P.now("vl1"))
            ev_prev = {}
            m_aw_of = {}
            aw_ev_of = {}

            def stage_a(hp):
                if hp >= 2:
                    P.wait("sync", "vv", aw_ev_of.get(hp - 2, 0))
                dd_d = dma("sync", dd_a[hp % 2], ddi[2 * hp],
                           ["d0", "d1"][hp % 2])
                dd_d2 = dma("sync", dd_b[hp % 2], ddi[2 * hp + 1],
                            ["d0", "d1"][hp % 2])
                bset = 2 * (hp % 2)
                P.wait("tensor", "vv", q_done[hp])
                if hp >= 2:
                    P.wait("tensor", "vv", aw_ev_of.get(hp - 2, 0))
                for p in range(2):
                    lo = 64 * p
                    tp = (64, 0) if p else None
                    awb = bank(bset + p)
                    mm(awb[:, 0:256],
                       kva[lo:lo + 64, 1024 * hp + 256 * tau:
                           1024 * hp + 256 * tau + 128],
                       xa_b[lo:lo + 64, 1024 * hp + 256 * tau:
                            1024 * hp + 256 * tau + 256], True, False, tp=tp)
                    mm(awb[:, 256:384],
                       kva[lo:lo + 64, 1024 * hp + 256 * tau + 128:
                           1024 * hp + 256 * tau + 256],
                       xa_b[lo:lo + 64, 1024 * hp + 256 * tau + 128:
                            1024 * hp + 256 * tau + 256], False, True, tp=tp)
                m_aw_of[hp] = P.now("mm")
                return dd_d2

            def stage_b(hp, dd_d2):
                bset = 2 * (hp % 2)
                pcb = 4 + (hp % 2)
                P.wait("vector", "mm", m_aw_of[hp])
                P.wait("vector", ["d0", "d1"][hp % 2], dd_d2)
                if hp >= 2:
                    P.wait("tensor", "vv", ev_prev.get(hp - 2, 0))
                for p in range(2):
                    b = bset + p
                    ddv = dd_a[hp % 2] if p == 0 else dd_b[hp % 2]
                    awt = awt_p[hp % 2][p]
                    v_aw = vv(lambda e, b=b, d=ddv, a=awt: e.tensor_tensor(
                        a, ps[:, 512 * b:512 * b + 384], d[:, 0:384],
                        ALU.mult))
                    if p == 1:
                        aw_ev_of[hp] = v_aw
                    P.wait("tensor", "vv", v_aw)
                    hd = 2 * hp + p
                    lo = 64 * p
                    tpc = (0, 64) if p else None
                    mm(ps[lo:lo + 64, 512 * pcb:512 * pcb + 256],
                       vsl[0][:, 64 * hd:64 * hd + 64], awt[:, 0:256],
                       True, False, tp=tpc)
                    mm(ps[lo:lo + 64, 512 * pcb + 128:512 * pcb + 256],
                       vsl[1][:, 64 * hd:64 * hd + 64], awt[:, 256:384],
                       False, False, tp=tpc)
                    tpi = (lo, lo) if p else None
                    mm(ps[lo:lo + 64, 512 * pcb:512 * pcb + 256],
                       s2t[lo:lo + 64, 2048 * slot + 64 * hd:
                           2048 * slot + 64 * hd + 64],
                       qq[lo:lo + 64, 256 * hp:256 * hp + 256],
                       False, True, tp=tpi)
                m_pc = P.now("mm")
                P.wait("vector", "mm", m_pc)
                ev_prev[hp] = vv(lambda e, b=pcb, hp=hp, h=h, k=k:
                                 e.tensor_copy(attnv(h, hp, 256 * k, 256),
                                               ps[:, 512 * b:512 * b + 256]))

            dd_ctx = {}
            dd_ctx[0] = stage_a(0)
            for hp in range(1, NCH):
                dd_ctx[hp] = stage_a(hp)
                stage_b(hp - 1, dd_ctx[hp - 1])
            stage_b(NCH - 1, dd_ctx[NCH - 1])
        P.wait("scalar", "vv", P.now("vv"))
        for c in range(NCH):
            if c >= 2:
                P.wait("scalar", "mm", mm_r[c - 2])
            s_q = sc(lambda e, c=c, h=h, s=sq_sl[c % 2]: e.activation(
                s, attnv(h, c, 0, 512), AF.Square))
            P.wait("tensor", "sc", s_q)
            mm(ps[0:1, 3072:3584], onescb[:, 0:1], sq_sl[c % 2],
               c == 0, c == NCH - 1)
            if c == 0:
                mm_r = {}
            mm_r[c] = P.now("mm")
        P.wait("vector", "mm", P.now("mm"))
        v_ms = vvw(lambda e: e.tensor_scalar(
            ss[0:1, 1024:1536], ps[0:1, 3072:3584], 1.0 / 2048.0, 1e-5,
            ALU.mult, ALU.add))
        P.wait("scalar", "vv", v_ms)
        s_r = scw(lambda e: e.activation(
            ss[0:1, 1536:2048], ss[0:1, 1024:1536], AF.Sqrt))
        P.wait("vector", "sc", s_r)
        v_rs = vvw(lambda e: e.reciprocal(arow[0:1, 0:512], ss[0:1, 1536:2048]))
        P.wait("tensor", "vv", v_rs)
        mm(bank(7), onesrf[0:1, :], arow[0:1, 0:512], True, True)
        P.wait("vector", "mm", P.now("mm"))
        for c in range(NCH):
            vvw(lambda e, c=c, h=h, t=ss[:, 0:512]: e.tensor_tensor(
                t, attnv(h, c, 0, 512), gav(c, TH * h, TH), ALU.mult))
            vv(lambda e, c=c, h=h, t=ss[:, 0:512]: e.scalar_tensor_tensor(
                gav(c, TH * h, TH), t, nwc[:, c:c + 1], bank(7),
                ALU.mult, ALU.mult))
        P.barrier()

    checkpoint("s6", ga, cast=True)
    def x2v(c, th):
        base = kva_f if th == 0 else xbf_f
        return base[:, 512 * c:512 * c + 512]

    o_rv = {}

    def oproj_pre(w):
        P.wait("sync", "vv", o_rv.get(("tt", w - 1), 0))
        for oc in range(2):
            for th in range(2):
                oca = 2 * w + oc
                o_rv[(w, oc, th)] = dma(
                    "sync", x0sl[2 * oc + th],
                    x1p[128 * oca:128 * oca + 128,
                        512 * th:512 * th + 512], ["r0", "r1"][w % 2])

    def oproj_evac(w, bset, m_w):
        P.wait("vector", "mm", m_w)
        P.wait("vector", ["r0", "r1"][w % 2], o_rv[(w, 1, 1)])
        last = 0
        for oc in range(2):
            for th in range(2):
                oca = 2 * w + oc
                last = vv(lambda e, b=bset + 2 * oc + th,
                          s=x0sl[2 * oc + th], d=x2v(oca, th):
                          e.tensor_tensor(
                              d, ps[:, 512 * b:512 * b + 512], s, ALU.add))
        o_rv[("tt", w)] = last
        return ("vv", last)

    win_phase(cast_fill(wo), lambda ic, th: gav(ic, 512 * th, 512), NCH,
              oproj_evac, pre=oproj_pre)
    P.barrier()

    checkpoint("s7")
    ln_stats_apply(lambda h, c: x2v(c, h),
                   lambda h, c: xa_b[:, 1024 * c + TH * h:
                                     1024 * c + TH * h + TH])
    P.barrier()

    x3sl = [ss[:, 512 * i:512 * i + 512] for i in range(4)]
    OQ = ["scalar", "sync", "scalar", "sync"]
    OS = ["o0", "o1", "o0", "o1"]
    out_dma_cnt = {}



    def ffn2_evac(w, bset, m_w):
        P.wait("vector", "mm", m_w)
        last = 0
        for oc in range(2):
            for th in range(2):
                oca = 2 * w + oc
                sl = 2 * oc + th
                osem = OS[sl]
                if out_dma_cnt.get(sl):
                    P.wait("vector", osem, out_dma_cnt[sl])
                last = vv(lambda e, b=bset + 2 * oc + th, oca=oca,
                          s=x2v(oca, th), o=x3sl[sl]:
                          e.scalar_tensor_tensor(
                              o, ps[:, 512 * b:512 * b + 512],
                              b05c[:, oca:oca + 1], s, ALU.add, ALU.add))
                P.wait(OQ[sl], "vv", last)
                out_dma_cnt[sl] = P.tag(
                    OQ[sl], lambda e, oca=oca, th=th, o=x3sl[sl]:
                    e.dma_start(out=yT[128 * oca:128 * oca + 128,
                                       512 * th:512 * th + 512], in_=o),
                    osem, 16)
        return ("vv", last)

    win_phase(wt_fill_q("gpsimd", "wt2"),
              lambda ic, th: xa_b[:, 1024 * ic + 512 * th:
                                  1024 * ic + 512 * th + 512],
              NCH, ffn2_evac, bias_on=True)
    P.wait("scalar", "o0", P.now("o0"))
    P.wait("scalar", "o1", P.now("o1"))

    _finish(P, nc, st)
    return nc


def _finish(P, nc, st):
    with st:
        with nc.Block() as block:
            @block.tensor
            def _(t):
                for f in P.streams["tensor"]:
                    f(t)

            @block.vector
            def _(v):
                for f in P.streams["vector"]:
                    f(v)

            @block.scalar
            def _(s):
                for f in P.streams["scalar"]:
                    f(s)

            @block.gpsimd
            def _(g):
                for f in P.streams["gpsimd"]:
                    f(g)

            @block.sync
            def _(sy):
                for f in P.streams["sync"]:
                    f(sy)


_NC_CACHE = None


def _tile_w(wT):
    return np.ascontiguousarray(
        wT.reshape(NCH, 128, 8, 256).transpose(2, 1, 0, 3).reshape(
            8, 128, 4096))


def host_inputs(x, qkv_w, out_w, gate_w, norm_w, ffn_w, bias):
    qkv3 = np.asarray(qkv_w).reshape(H, 3, D, DM)
    wq_h = _tile_w(qkv3[:, 0].reshape(DM, DM).T)
    wk_h = _tile_w(qkv3[:, 1].reshape(DM, DM).T)
    wv_h = _tile_w(qkv3[:, 2].reshape(DM, DM).T)
    wg_h = _tile_w(np.asarray(gate_w).T)
    wo_h = _tile_w(np.asarray(out_w).T)
    wf_h = _tile_w(np.asarray(ffn_w).T)
    biasr_h = np.ascontiguousarray(np.asarray(bias).reshape(NCH, 128).T)
    nwc_h = np.ascontiguousarray(np.asarray(norm_w).reshape(NCH, 128).T)
    ddt, qdt, kdc, bdr, psrs = host_constants()
    ident = np.eye(128, dtype=BFNP)
    in_maps = []
    for c in range(8):
        b, t0 = c // 4, (c % 4) * T
        xT_h = np.ascontiguousarray(np.asarray(x)[b, t0:t0 + T, :].T)
        in_maps.append({
            "xT": xT_h, "wq": wq_h, "wk": wk_h, "wv": wv_h, "wg": wg_h,
            "wo": wo_h, "wf": np.ascontiguousarray(wf_h[c:c + 1]),
            "biasr": biasr_h, "nwi": nwc_h,
            "ddi": ddt, "qdi": qdt, "kdi": kdc, "bdi": bdr,
            "psi": psrs[c], "idi": ident,
        })
    return in_maps


def kernel(L0_attn_hidden_states, L0_attn_qkv_w, L0_attn_out_w,
           L0_attn_gate_w, L0_attn_norm_w, L0_ffn_weight, L0_ffn_bias):
    global _NC_CACHE
    if _NC_CACHE is None:
        _NC_CACHE = build_program()
    nc = _NC_CACHE
    in_maps = host_inputs(L0_attn_hidden_states, L0_attn_qkv_w,
                          L0_attn_out_w, L0_attn_gate_w, L0_attn_norm_w,
                          L0_ffn_weight, L0_ffn_bias)
    res = run_bass_kernel_spmd(nc, in_maps, core_ids=list(range(8)))
    out = np.zeros((2, 4096, DM), np.float32)
    for c in range(8):
        b, t0 = c // 4, (c % 4) * T
        out[b, t0:t0 + T, :] = np.asarray(res.results[c]["yT"]).T
    return out




# version 12
# speedup vs baseline: 1.0607x; 1.0607x over previous
import numpy as np
import ml_dtypes
from contextlib import ExitStack

import concourse.bass as bass
import concourse.mybir as mybir
from concourse.bass_utils import run_bass_kernel_spmd

F32, BF16 = mybir.dt.float32, mybir.dt.bfloat16
AF, ALU = mybir.ActivationFunctionType, mybir.AluOpType
AX = mybir.AxisListType.X
BFNP = ml_dtypes.bfloat16

H, D, BS, DM = 32, 64, 256, 2048
T, TH, NB = 1024, 512, 4
NCH = 16

ENGS = ("tensor", "vector", "scalar", "gpsimd", "sync")
SEMS = ["sx", "sw", "gd", "t0", "t1", "r0", "r1", "pk0", "pk1", "q0", "q1",
        "d0", "d1", "vl0", "vl1", "ms", "mq", "o0", "o1", "wt",
        "mm", "vv", "sc", "gp", "cc", "bar", "wt2", "pg0", "pg1",
        "cs0", "cw0"]


def host_constants():
    base = np.float32(1.0) / np.float32(2.0) ** np.float32(8.0 / H)
    rate = base ** (np.arange(H, dtype=np.float32) + 1.0)
    factor = np.float32(1.0 - 0.0 / (32 - 1 + 1e-05) + 1e-05)
    slope = (rate * factor).astype(np.float32)
    r = np.arange(BS, dtype=np.float32) + 1.0
    qd = np.exp(-slope[:, None] * r[None, :]).astype(np.float32)
    kd = np.exp(-slope[:, None] * (BS - r[None, :])).astype(np.float32)
    arg = slope[:, None, None] * (r[:, None] - r[None, :])[None]
    dd = np.where(arg >= 0, np.exp(-np.abs(arg)), 0.0).astype(np.float32)
    bd = np.exp(-slope * np.float32(BS)).astype(np.float32)

    ddt = np.zeros((H, 128, 384), np.float32)
    for h in range(H):
        dT = dd[h].T
        ddt[h, :, 0:256] = dT[0:128, :]
        ddt[h, :, 256:384] = dT[128:256, 128:256]
    qdt = np.zeros((NCH, 128, 256), np.float32)
    for c in range(NCH):
        for p in range(128):
            qdt[c, p, :] = qd[2 * c + p // 64]
    kdc = np.zeros((128, 64), np.float32)
    for h in range(H):
        for jt in range(2):
            kdc[:, 2 * h + jt] = kd[h, 128 * jt:128 * jt + 128]
    bdr = np.tile(np.repeat(bd, D)[None, :], (128, 1)).astype(np.float32)
    psrs = []
    for c in range(8):
        psm = np.zeros((8, 2048), np.float32)
        g0 = (c // 4) * 4
        for j in range(g0, c):
            psm[j] = np.repeat(bd ** (4 * (c - 1 - j)), D)
        psrs.append(psm)
    return ddt.astype(BFNP), qdt.astype(BFNP), kdc, \
        bdr.astype(BFNP), psrs


class Prog:
    def __init__(self):
        self.streams = {k: [] for k in ENGS}
        self.cnt = {}
        self.sems = None
        self.done = False

    def raw(self, eng, fn):
        if self.done:
            return
        self.streams[eng].append(fn)

    def tag(self, eng, fn, sem, n=1):
        if self.done:
            return self.cnt.get(sem, 0)
        self.cnt[sem] = self.cnt.get(sem, 0) + n
        tgt = self.cnt[sem]
        P = self

        def wrapped(e, fn=fn, sem=sem, n=n):
            fn(e).then_inc(P.sems[sem], n)
        self.streams[eng].append(wrapped)
        return tgt

    def wait(self, eng, sem, val):
        if self.done or val is None or val <= 0:
            return
        P = self

        def w(e, sem=sem, val=val):
            e.wait_ge(P.sems[sem], val)
        self.streams[eng].append(w)

    def now(self, sem):
        return self.cnt.get(sem, 0)

    def barrier(self):
        base = self.cnt.get("bar", 0)
        n = len(ENGS)
        for eng in ENGS:
            self.tag(eng, lambda e: e.drain(), "bar")
        for eng in ENGS:
            self.wait(eng, "bar", base + n)


def build_program(stage=None):
    nc = bass.Bass("TRN2", target_bir_lowering=False,
                   detect_race_conditions=False)
    dt = nc.dram_tensor
    xT = dt("xT", [DM, T], F32, kind="ExternalInput").ap()
    wq = dt("wq", [8, 128, 4096], F32, kind="ExternalInput").ap()
    wk = dt("wk", [8, 128, 4096], F32, kind="ExternalInput").ap()
    wv = dt("wv", [8, 128, 4096], F32, kind="ExternalInput").ap()
    wg = dt("wg", [8, 128, 4096], F32, kind="ExternalInput").ap()
    wo = dt("wo", [8, 128, 4096], F32, kind="ExternalInput").ap()
    wf = dt("wf", [1, 128, 4096], F32, kind="ExternalInput").ap()
    biasr = dt("biasr", [128, NCH], F32, kind="ExternalInput").ap()
    nwi = dt("nwi", [128, NCH], F32, kind="ExternalInput").ap()
    ddi = dt("ddi", [H, 128, 384], BF16, kind="ExternalInput").ap()
    qdi = dt("qdi", [NCH, 128, 256], BF16, kind="ExternalInput").ap()
    kdi = dt("kdi", [128, 64], F32, kind="ExternalInput").ap()
    bdi = dt("bdi", [128, DM], BF16, kind="ExternalInput").ap()
    psi = dt("psi", [8, DM], F32, kind="ExternalInput").ap()
    idi = dt("idi", [128, 128], BF16, kind="ExternalInput").ap()
    yT = dt("yT", [DM, T], F32, kind="ExternalOutput").ap()
    x1p = dt("x1p", [DM, T], F32).ap()
    vp = dt("vp", [8, 8, 128, 256], BF16).ap()
    dsc = dt("dsc", [128, 1], F32).ap()
    cci = dt("cci", [64, DM], F32)
    cco = dt("cco", [512, DM], F32)
    ccs_i = dt("ccs_i", [1, 1], F32)
    ccs_o = dt("ccs_o", [1, 1], F32)
    ccw_i = dt("ccw_i", [128, 4096], BF16)
    wtg_t = dt("wtg", [1024, 4096], BF16)
    wtg = wtg_t.ap()
    cw_hold = []

    P = Prog()
    st = ExitStack()
    e = st.enter_context
    xa_f = e(nc.sbuf_tensor("xa", [128, 8192], F32))
    xa_b = xa_f.ap().bitcast(BF16)
    kva_t = e(nc.sbuf_tensor("kva", [128, 16384], BF16))
    kva = kva_t.ap()
    kva_f = kva.bitcast(F32)
    ga_t = e(nc.sbuf_tensor("ga", [128, 16384], BF16))
    ga = ga_t.ap()
    xbf_t = e(nc.sbuf_tensor("xbf", [128, 16384], BF16))
    xbf = xbf_t.ap()
    xbf_f = xbf.bitcast(F32)
    va = e(nc.sbuf_tensor("va", [128, 6144], BF16))
    ww = e(nc.sbuf_tensor("ww", [128, 4096], BF16))
    ss_t = e(nc.sbuf_tensor("ss", [128, 3072], F32))
    ss = ss_t.ap()
    ss_b = ss.bitcast(BF16)
    s2t = e(nc.sbuf_tensor("s2t", [128, 4096], BF16))
    cs = e(nc.sbuf_tensor("cs", [128, 4096], BF16))
    sm = e(nc.sbuf_tensor("sm", [128, DM], F32))
    bdr_s = e(nc.sbuf_tensor("bdr_s", [128, DM], BF16))
    qdr = e(nc.sbuf_tensor("qdr", [128, 4096], BF16))
    kdc_s = e(nc.sbuf_tensor("kdc_s", [128, 64], F32))
    idb = e(nc.sbuf_tensor("idb", [128, 128], BF16))
    nwc = e(nc.sbuf_tensor("nwc", [128, NCH], F32))
    acc = e(nc.sbuf_tensor("acc", [128, 2], F32))
    thrc = e(nc.sbuf_tensor("thrc", [128, 1], F32))
    onescf = e(nc.sbuf_tensor("onescf", [128, 1], F32))
    onescb = e(nc.sbuf_tensor("onescb", [128, 1], BF16))
    onesrf = e(nc.sbuf_tensor("onesrf", [1, 128], F32))
    onesrb = e(nc.sbuf_tensor("onesrb", [1, 512], BF16))
    b05c = e(nc.sbuf_tensor("b05c", [128, NCH], F32))
    arow = e(nc.sbuf_tensor("arow", [1, 1024], F32))
    brow = e(nc.sbuf_tensor("brow", [1, 1024], F32))
    s05v = e(nc.sbuf_tensor("s05v", [1, 1], F32))
    s05c = e(nc.sbuf_tensor("s05c", [128, 1], F32))
    ps_t = e(nc.psum_tensor("ps", [128, 4096], F32))
    ps = ps_t.ap()
    psb = ps.bitcast(BF16)
    sems = {n: e(nc.semaphore(n)) for n in SEMS}
    P.sems = sems

    def bank(i, w=512):
        return ps[:, 512 * i: 512 * i + w]

    def checkpoint(name, dump_ap=None, cast=False):
        if stage == name:
            P.barrier()
            if dump_ap is not None:
                if cast:
                    raw = dump_ap.bitcast(F32) if dump_ap.dtype == BF16 else dump_ap
                    n = dma("sync",
                            yT[0:1024, :].rearrange("(a p) t -> p a t", p=128),
                            raw.rearrange("p (a t) -> p a t", a=8), "o0")
                    n = dma("scalar",
                            yT[1024:2048, :].rearrange("(a p) t -> p a t",
                                                       p=128),
                            raw.rearrange("p (a t) -> p a t", a=8), "o0")
                else:
                    n = dma("sync", yT.rearrange("(a p) t -> p a t", p=128),
                            dump_ap, "o0")
                for eng in ENGS:
                    P.wait(eng, "o0", n)
            P.done = True

    def dma(q, out, in_, sem):
        return P.tag(q, lambda e, o=out, i=in_: e.dma_start(out=o, in_=i),
                     sem, 16)

    def mm(out, lhsT, rhs, start, stop, tp=None):
        return P.tag("tensor",
                     lambda e, o=out, l=lhsT, r=rhs, s=start, p=stop, t=tp:
                     e.matmul(o, l, r, start=s, stop=p, tile_position=t),
                     "mm")

    def vv(fn):
        return P.tag("vector", fn, "vv")

    def vvw(fn):
        t = P.tag("vector", fn, "vv")
        P.wait("vector", "vv", t)
        return t

    def sc(fn):
        return P.tag("scalar", fn, "sc")

    def scw(fn):
        t = P.tag("scalar", fn, "sc")
        P.wait("scalar", "sc", t)
        return t

    def xav(c):
        return xa_f[:, 512 * c: 512 * c + 512]

    def qv(c, t0, n):
        return xa_b[:, 1024 * c + t0: 1024 * c + t0 + n]

    def kv(c, t0, n):
        return kva[:, 1024 * c + t0: 1024 * c + t0 + n]

    def gav(c, t0, n):
        return ga[:, 1024 * c + t0: 1024 * c + t0 + n]

    def xbfv(c, t0, n):
        return xbf[:, 1024 * c + t0: 1024 * c + t0 + n]

    def attnv(h, c, i0, n):
        return xbf[:, 8192 * h + 512 * c + i0: 8192 * h + 512 * c + i0 + n]

    WBUFS = None

    def wwv(w, ic):
        buf = ww if w % 2 == 0 else qdr.ap()
        return buf[:, 256 * ic: 256 * ic + 256]

    dma("sync", bdr_s.ap(), bdi, "sx")
    dma("sync", kdc_s.ap(), kdi, "sx")
    dma("sync", idb.ap(), idi, "sx")
    dma("sync", nwc.ap(), nwi, "sx")
    c_s0 = dma("sync", ss[:, 2080:2080 + NCH], biasr, "ms")
    P.raw("gpsimd", lambda e: e.memset(onescf.ap(), 1.0))
    P.raw("gpsimd", lambda e: e.memset(onescb.ap(), 1.0))
    P.raw("gpsimd", lambda e: e.memset(onesrf.ap(), 1.0))
    P.raw("gpsimd", lambda e: e.memset(onesrb.ap(), 1.0))
    P.tag("gpsimd", lambda e: e.memset(sm.ap(), 0.0), "gp")
    P.wait("vector", "ms", c_s0)
    P.wait("vector", "gp", P.now("gp"))
    vv_b05 = vvw(lambda e: e.tensor_scalar(
        b05c.ap(), ss[:, 2080:2080 + NCH], 0.5, None, ALU.mult))
    v_ms0 = vv(lambda e: e.memset(acc.ap(), 0.0))
    P.wait("vector", "vv", v_ms0)

    kva_f2 = kva.bitcast(F32)
    SLAB = kva_f2[:, 0:4096]
    va_f = va.ap().bitcast(F32)
    d = dma("gpsimd", SLAB, wf[0], "t0")
    P.wait("vector", "t0", d)
    v_r = vv(lambda e: e.reduce_sum(acc[:, 0:1], SLAB, axis=AX,
                                    apply_absolute_value=True))
    P.wait("sync", "vv", v_r)
    d_a1 = dma("sync", dsc, acc[:, 0:1], "ms")
    P.wait("sync", "ms", d_a1)
    d_ac = dma("sync", va_f[0:1, 0:128], dsc.rearrange("p o -> o p"), "ms")
    P.wait("vector", "ms", d_ac)
    v_p = vvw(lambda e: e.reduce_sum(va_f[0:1, 128:129], va_f[0:1, 0:128],
                                     axis=AX))
    P.wait("gpsimd", "vv", v_p)
    d_ci0 = dma("gpsimd", ccs_i.ap(), va_f[0:1, 128:129], "gd")
    P.wait("gpsimd", "gd", d_ci0)
    cs0_t = P.tag("gpsimd", lambda e: e.collective_compute(
        "AllReduce", ALU.add, replica_groups=[list(range(8))],
        ins=[ccs_i.ap()], outs=[ccs_o.ap()]), "cs0")
    P.wait("tensor", "gp", P.now("gp"))

    gaf = ga.bitcast(F32)

    def ln_stats_apply(src_view, dst_view, mid_hook=None):
        cast_sl = [ss_b[:, 4096 + 512 * i:4608 + 512 * i] for i in range(4)]
        sq_sl = [cs[:, 512 * i:512 * i + 512] for i in range(4)]
        sumc = {0: 3072, 1: 2048}
        sqc = {0: 3584, 1: 2560}
        mm_ch = {}
        for i in range(2 * NCH):
            h, c = i // NCH, i % NCH
            if i >= 4:
                P.wait("scalar", "mm", mm_ch[i - 4])
                P.wait("vector", "mm", mm_ch[i - 4])
            s_c = sc(lambda e, h=h, c=c, s=cast_sl[i % 4]:
                     e.copy(s, src_view(h, c)))
            s_q = vv(lambda e, h=h, c=c, q=sq_sl[i % 4]:
                     e.tensor_tensor(q, src_view(h, c), src_view(h, c),
                                     ALU.mult))
            P.wait("tensor", "sc", s_c)
            mm(ps[0:1, sumc[h]:sumc[h] + 512], onescb[:, 0:1],
               cast_sl[i % 4], c == 0, c == NCH - 1)
            P.wait("tensor", "vv", s_q)
            mm(ps[0:1, sqc[h]:sqc[h] + 512], onescb[:, 0:1],
               sq_sl[i % 4], c == 0, c == NCH - 1)
            mm_ch[i] = P.now("mm")
        if mid_hook is not None:
            mid_hook()
        P.wait("vector", "mm", P.now("mm"))
        mrow = ss[0:1, 1024:2048]
        qrow = ss[0:1, 2048:3072]
        trow = ss[0:1, 0:1024]
        vvw(lambda e: e.tensor_scalar(ss[0:1, 1024:1536], ps[0:1, 3072:3584],
                                      1.0 / 2048.0, None, ALU.mult))
        vvw(lambda e: e.tensor_scalar(ss[0:1, 1536:2048], ps[0:1, 2048:2560],
                                      1.0 / 2048.0, None, ALU.mult))
        vvw(lambda e: e.tensor_scalar(ss[0:1, 2048:2560], ps[0:1, 3584:4096],
                                      1.0 / 2048.0, None, ALU.mult))
        vvw(lambda e: e.tensor_scalar(ss[0:1, 2560:3072], ps[0:1, 2560:3072],
                                      1.0 / 2048.0, None, ALU.mult))
        vvw(lambda e: e.tensor_tensor(trow, mrow, mrow, ALU.mult))
        vvw(lambda e: e.scalar_tensor_tensor(
            trow, trow, -1.0, qrow, ALU.mult, ALU.add))
        v_var = vvw(lambda e: e.tensor_scalar(trow, trow, 1e-5, None,
                                              ALU.add))
        P.wait("scalar", "vv", v_var)
        s_sq = scw(lambda e: e.activation(qrow, trow, AF.Sqrt))
        P.wait("vector", "sc", s_sq)
        vvw(lambda e: e.reciprocal(arow[0:1, 0:1024], qrow))
        v_ab = vvw(lambda e: e.scalar_tensor_tensor(
            brow[0:1, 0:1024], mrow, -1.0, arow[0:1, 0:1024],
            ALU.mult, ALU.mult))
        P.wait("tensor", "vv", v_ab)
        AB = {0: (4, 5), 1: (2, 3)}
        mm(bank(4), onesrf[0:1, :], arow[0:1, 0:512], True, True)
        mm(bank(5), onesrf[0:1, :], brow[0:1, 0:512], True, True)
        mm(bank(2), onesrf[0:1, :], arow[0:1, 512:1024], True, True)
        mm(bank(3), onesrf[0:1, :], brow[0:1, 512:1024], True, True)
        P.wait("vector", "mm", P.now("mm"))
        tmp_sl = [ss[:, 0:512], ss[:, 512:1024]]
        for i in range(2 * NCH):
            h, c = i // NCH, i % NCH
            vvw(lambda e, h=h, c=c, t=tmp_sl[i % 2], a=AB[h][0]:
                e.tensor_tensor(t, src_view(h, c), bank(a), ALU.mult))
            vv(lambda e, h=h, c=c, t=tmp_sl[i % 2], b=AB[h][1]:
               e.tensor_tensor(dst_view(h, c), t, bank(b), ALU.add))
        return P.now("vv")

    checkpoint("s0")
    tern_tn = []

    def emit_tern_body():
        P.wait("sync", "cs0", cs0_t)
        d_so = dma("sync", va_f[0:1, 128:129], ccs_o.ap(), "ms")
        P.wait("vector", "ms", d_so)
        vvw(lambda e: e.tensor_scalar(
            s05v[0:1, 0:1], va_f[0:1, 128:129], 0.5 / 4194304.0, None, ALU.mult))
        v_t0 = vvw(lambda e: e.tensor_scalar(
            va_f[0:1, 129:130], va_f[0:1, 128:129], 0.5 / 4194304.0, 5e-11,
            ALU.mult, ALU.add))
        v_t0b = vvw(lambda e: e.tensor_copy(thrc[0:1, 0:1], va_f[0:1, 129:130]))
        v_t0c = vvw(lambda e: e.tensor_copy(s05c[0:1, 0:1], s05v[0:1, 0:1]))
        P.wait("sync", "vv", v_t0c)
        k = 1
        while k < 128:
            dma("sync", thrc[k:2 * k, 0:1], thrc[0:k, 0:1], "ms")
            dma("sync", s05c[k:2 * k, 0:1], s05c[0:k, 0:1], "ms")
            P.wait("sync", "ms", P.now("ms"))
            k *= 2
        P.wait("vector", "ms", P.now("ms"))
        vvw(lambda e: e.tensor_scalar(acc[:, 1:2], thrc[:, 0:1], -1.0, None,
                                      ALU.mult))
        vv(lambda e: e.tensor_scalar(
            ss_b[:, 0:4096], SLAB, thrc[:, 0:1], s05c[:, 0:1],
            ALU.is_gt, ALU.mult))
        vvw(lambda e: e.tensor_scalar(
            ga[:, 0:4096], SLAB, acc[:, 1:2], s05c[:, 0:1],
            ALU.is_lt, ALU.mult))
        tn = vv(lambda e: e.tensor_sub(
            ga[:, 4096:8192], ss_b[:, 0:4096], ga[:, 0:4096]))
        P.wait("gpsimd", "vv", tn)
        d_cw = dma("gpsimd", ccw_i.ap(), ga[:, 4096:8192], "wt")
        P.wait("gpsimd", "wt", d_cw)
        cw_hold.append(P.tag("gpsimd", lambda e: e.collective_compute(
            "AllGather", ALU.bypass, replica_groups=[list(range(8))],
            ins=[ccw_i.ap()], outs=[wtg_t.ap()]), "cw0"))

        tern_tn.append(tn)

    def h1v(c):
        base = gaf if c < 8 else kva_f2
        cc = c % 8
        return base[:, 4096 + 512 * cc:4608 + 512 * cc]

    def ln_src(h, c):
        return xav(c) if h == 0 else h1v(c)

    for c in range(NCH):
        dma("sync", xav(c), xT[128 * c:128 * c + 128, 0:TH], "sx")
        dma("scalar", h1v(c), xT[128 * c:128 * c + 128, TH:T], "sx")
    P.wait("scalar", "sx", P.now("sx"))
    P.wait("vector", "sx", P.now("sx"))

    def emit_tern():
        emit_tern_body()

    ln_stats_apply(ln_src, lambda h, c: xbfv(c, TH * h, TH),
                   mid_hook=emit_tern)
    P.barrier()
    checkpoint("s1h0", xbf, cast=True)

    def win_phase(fill, rhs_view, n_ic, evac, bias_on=False, pre=None):
        guards = {}
        mcnt = {}
        for w in range(8):
            rsem, rval = fill(w, mcnt.get(w - 2, 0))
            if pre is not None:
                pre(w)
            P.wait("tensor", rsem, rval)
            if w >= 2:
                gsem, gval = guards[w - 2]
                P.wait("tensor", gsem, gval)
            bset = 4 * (w % 2)
            for ic in range(n_ic):
                for oc in range(2):
                    for th in range(2):
                        mm(bank(bset + 2 * oc + th),
                           wwv(w, ic)[:, 128 * oc:128 * oc + 128],
                           rhs_view(ic, th), ic == 0, ic == n_ic - 1)
            mcnt[w] = P.now("mm")
            guards[w] = evac(w, bset, mcnt[w])
        for w in (6, 7):
            gsem, gval = guards[w]
            P.wait("tensor", gsem, gval)
        return mcnt

    checkpoint("s1", xbf, cast=True)
    x0sl = [ss[:, 512 * i:512 * i + 512] for i in range(4)]
    x1sl = [ss[:, 2048:2560], ss[:, 2560:3072]]

    def wt_fill_q(q, sem):
        def fill(w, prev_mm):
            P.wait(q, "mm", prev_mm)
            if w == 0:
                P.wait(q, "cw0", cw_hold[0])
            buf = ww if w % 2 == 0 else qdr.ap()
            d = dma(q, buf[:, 0:4096], wtg[128 * w:128 * w + 128, :], sem)
            return (sem, d)
        return fill

    def ffn_pre_evac(res_src, out_dst):
        RS = ["r0", "r1"]
        PK = ["pg0", "pg1"]
        rv_of = {}

        def pre(w):
            P.wait("sync", "vv", rv_of.get(("tt", w - 1), 0))
            for oc in range(2):
                for th in range(2):
                    oca = 2 * w + oc
                    rv_of[(w, oc, th)] = dma(
                        "sync", x0sl[2 * oc + th],
                        res_src[128 * oca:128 * oca + 128,
                                512 * th:512 * th + 512], RS[w % 2])

        def evac(w, bset, m_w):
            P.wait("vector", "mm", m_w)
            P.wait("vector", RS[w % 2], rv_of[(w, 1, 1)])
            last = 0
            for oc in range(2):
                for th in range(2):
                    oca = 2 * w + oc
                    sl = th
                    P.wait("vector", PK[sl], P.now(PK[sl]))
                    last = vv(lambda e, b=bset + 2 * oc + th, oca=oca,
                              s=x0sl[2 * oc + th], o=x1sl[sl]:
                              e.scalar_tensor_tensor(
                                  o, ps[:, 512 * b:512 * b + 512],
                                  b05c[:, oca:oca + 1], s, ALU.add,
                                  ALU.add))
                    P.wait("gpsimd", "vv", last)
                    dma("gpsimd",
                        out_dst[128 * oca:128 * oca + 128,
                                512 * th:512 * th + 512], x1sl[sl], PK[sl])
            rv_of[("tt", w)] = last
            return ("vv", last)
        return pre, evac

    f1pre, f1evac = ffn_pre_evac(xT, x1p)
    P.wait("sync", "vv", tern_tn[0])
    win_phase(wt_fill_q("scalar", "sw"), lambda ic, th: xbfv(ic, 512 * th, 512),
              NCH, f1evac, bias_on=True, pre=f1pre)
    P.barrier()

    checkpoint("s2")
    for q in ("sync", "scalar"):
        P.wait(q, "pg0", P.now("pg0"))
        P.wait(q, "pg1", P.now("pg1"))
    for c in range(NCH):
        dma("sync", xav(c), x1p[128 * c:128 * c + 128, 0:TH], "sx")
        dma("scalar", h1v(c), x1p[128 * c:128 * c + 128, TH:T], "sx")
    P.wait("scalar", "sx", P.now("sx"))
    P.wait("vector", "sx", P.now("sx"))
    ln_stats_apply(ln_src, lambda h, c: xbfv(c, TH * h, TH))
    P.barrier()

    checkpoint("s3", xbf, cast=True)
    def cast_fill(wsrc):
        def fill(w, prev_mm):
            P.wait("gpsimd", "mm", prev_mm)
            buf = ww if w % 2 == 0 else qdr.ap()
            d = dma("gpsimd", buf[:, 0:4096], wsrc[w], "gd")
            return ("gd", d)
        return fill

    def silu_evac(dst_view, act):
        def evac(w, bset, m_w):
            P.wait("scalar", "mm", m_w)
            last = 0
            for oc in range(2):
                for th in range(2):
                    last = sc(lambda e, b=bset + 2 * oc + th,
                              d=dst_view(2 * w + oc, th): e.activation(
                                  d, ps[:, 512 * b:512 * b + 512], act))
            return ("sc", last)
        return evac

    win_phase(cast_fill(wk), lambda ic, th: xbfv(ic, 512 * th, 512), NCH,
              silu_evac(lambda oca, th: kv(oca, 512 * th, 512), AF.Silu))
    P.barrier()

    checkpoint("s4a", kva, cast=True)
    kkd_bufs = [ss_b[:, 0:2048], ss_b[:, 2048:4096]]
    vsl = [va[:, 0:2048], va[:, 2048:4096]]
    PKS = ["pk0", "pk1"]
    ev_v = {}
    mv = {}
    pk_tot = {0: 0, 1: 0}
    for w in range(8):
        P.wait("gpsimd", "mm", mv.get(w - 2, 0))
        buf = ww if w % 2 == 0 else qdr.ap()
        d = dma("gpsimd", buf[:, 0:4096], wv[w], "gd")
        P.wait("tensor", "gd", d)
        if w >= 2:
            P.wait("tensor", "sc", ev_v[w - 2])
        bset = 4 * (w % 2)
        for tt in range(8):
            bk, off = bset + tt % 4, 256 * (tt // 4)
            for ic in range(NCH):
                mm(ps[:, 512 * bk + off:512 * bk + off + 256],
                   xbfv(ic, 128 * tt, 128), wwv(w, ic), ic == 0,
                   ic == NCH - 1)
        mv[w] = P.now("mm")
        P.wait("scalar", "mm", mv[w])
        P.wait("scalar", PKS[w % 2], pk_tot[w % 2])
        last = 0
        for tt in range(8):
            bk, off = bset + tt % 4, 256 * (tt // 4)
            scol = 256 * (8 * (w % 2) + tt)
            last = sc(lambda e, b=bk, o=off, sc0=scol: e.activation(
                va[:, sc0:sc0 + 256], ps[:, 512 * b + o:512 * b + o + 256],
                AF.Silu))
            P.wait("sync", "sc", last)
            dma("sync", vp[tt, w], va[:, scol:scol + 256], PKS[w % 2])
        ev_v[w] = last
        pk_tot[w % 2] = P.now(PKS[w % 2])
    P.barrier()

    c_started = set()
    prev_cmm = 0
    for tt in range(8):
        tau, jt = tt // 2, tt % 2
        dstv = kkd_bufs[tt % 2]
        P.wait("sync", "mm", prev_cmm)
        vld = dma("sync", vsl[tt % 2].rearrange("p (w j) -> p w j", w=8),
                  vp[tt].rearrange("w p j -> p w j"),
                  ["vl0", "vl1"][tt % 2])
        kkd_ev = {}
        for c in range(NCH):
            bk = c % 2
            tp_dst = psb[:, 1024 * bk:1024 * bk + 128]
            if c >= 2:
                P.wait("tensor", "vv", kkd_ev[c - 2])
            P.tag("tensor", lambda e, d=tp_dst, c=c, tau=tau, jt=jt:
                  e.matmul(d, kv(c, 256 * tau + 128 * jt, 128), idb.ap(),
                           start=True, stop=True, is_transpose=True), "mm")
            m = P.now("mm")
            P.wait("vector", "mm", m)
            for p in range(2):
                hd = 2 * c + p
                kkd_ev[c] = vv(lambda e, d=tp_dst, c=c, p=p, hd=hd, jt=jt,
                               dv=dstv: e.tensor_scalar(
                    dv[:, 128 * c + 64 * p:128 * c + 64 * p + 64],
                    d[:, 64 * p:64 * p + 64],
                    kdc_s[:, 2 * hd + jt:2 * hd + jt + 1], None, ALU.mult))
        kkd_done = P.now("vv")
        P.wait("tensor", "vv", kkd_done)
        P.wait("tensor", ["vl0", "vl1"][tt % 2], vld)
        rhalf = 64 * (tau // 2)
        for hd in range(32):
            cbank = 4 + (64 * hd) // 512
            key = (cbank, tau)
            first = key not in c_started
            c_started.add(key)
            last_c = (jt == 1) and (hd % 8 == 7)
            tp = (0, 64) if rhalf else None
            mm(ps[rhalf:rhalf + 64, 2048 + 64 * hd:2048 + 64 * hd + 64],
               dstv[:, 128 * (hd // 2) + 64 * (hd % 2):
                    128 * (hd // 2) + 64 * (hd % 2) + 64],
               vsl[tt % 2][:, 64 * hd:64 * hd + 64],
               first, last_c, tp=tp)
        prev_cmm = P.now("mm")
        if jt == 1:
            P.wait("vector", "mm", prev_cmm)
            cv = vv(lambda e, r=rhalf, tau=tau: e.tensor_copy(
                cs[r:r + 64, 2048 * (tau % 2):2048 * (tau % 2) + 2048],
                ps[r:r + 64, 2048:4096]))
            P.wait("tensor", "vv", cv)
    P.barrier()
    checkpoint("s4b")
    dma("sync", ss_b[0:64, 0:2048], cs[64:128, 0:2048], "ms")
    d_c3 = dma("sync", ss_b[0:64, 4096:6144], cs[64:128, 2048:4096], "ms")
    P.wait("vector", "ms", d_c3)
    Pv = ps[0:64, 0:2048]
    vvw(lambda e: e.tensor_copy(Pv, cs[0:64, 0:2048]))
    vvw(lambda e: e.tensor_tensor(Pv, Pv, bdr_s[0:64, :], ALU.mult))
    vvw(lambda e: e.tensor_add(Pv, Pv, cs[0:64, 2048:4096]))
    vvw(lambda e: e.tensor_tensor(Pv, Pv, bdr_s[0:64, :], ALU.mult))
    vvw(lambda e: e.tensor_add(Pv, Pv, ss_b[0:64, 0:2048]))
    vvw(lambda e: e.tensor_tensor(Pv, Pv, bdr_s[0:64, :], ALU.mult))
    v_pub = vvw(lambda e: e.tensor_add(ss[0:64, 0:2048], Pv,
                                       ss_b[0:64, 4096:6144]))
    P.wait("gpsimd", "vv", v_pub)
    d_ci = dma("gpsimd", cci.ap(), ss[0:64, 0:2048], "gd")
    P.wait("gpsimd", "gd", d_ci)
    cc_t = P.tag("gpsimd", lambda e: e.collective_compute(
        "AllGather", ALU.bypass, replica_groups=[list(range(8))],
        ins=[cci.ap()], outs=[cco.ap()]), "cc")
    P.barrier()

    checkpoint("s4c")
    win_phase(cast_fill(wq), lambda ic, th: xbfv(ic, 512 * th, 512), NCH,
              silu_evac(lambda oca, th: qv(oca, 512 * th, 512), AF.Silu))
    P.barrier()
    win_phase(cast_fill(wg), lambda ic, th: xbfv(ic, 512 * th, 512), NCH,
              silu_evac(lambda oca, th: gav(oca, 512 * th, 512), AF.Sigmoid))
    P.barrier()

    checkpoint("s5", xa_b, cast=True)
    P.wait("gpsimd", "cc", cc_t)
    P.barrier()
    dma("sync", qdr.ap().rearrange("p (a t) -> p a t", a=16),
        qdi.rearrange("a p t -> p a t"), "sx")
    P.wait("vector", "sx", P.now("sx"))
    va_f = va.ap().bitcast(F32)
    for j in range(7):
        dj = dma("scalar", va_f[0:1, 0:DM], psi[j:j + 1, :], "mq")
        ds = dma("sync", ss[0:64, 1024:3072],
                 cco.ap()[64 * j:64 * j + 64, :], "ms")
        P.wait("tensor", "mq", dj)
        for q4 in range(4):
            mm(ps[0:64, 2048 + 512 * q4:2048 + 512 * q4 + 512],
               onesrf[0:1, 0:64], va_f[0:1, 512 * q4:512 * q4 + 512],
               True, True)
        P.wait("vector", "mm", P.now("mm"))
        P.wait("vector", "ms", ds)
        vvw(lambda e: e.tensor_tensor(ps[0:64, 0:2048], ss[0:64, 1024:3072],
                                      ps[0:64, 2048:4096], ALU.mult))
        v_j = vvw(lambda e: e.tensor_add(sm[0:64, :], sm[0:64, :],
                                         ps[0:64, 0:2048]))
        P.wait("tensor", "vv", v_j)
        P.wait("sync", "vv", v_j)
        P.wait("scalar", "vv", v_j)
    P.barrier()

    checkpoint("s6a")
    awt_p = [[ss_b[:, 4096:4480], ss_b[:, 4480:4864]],
             [ss_b[:, 4864:5248], ss_b[:, 5248:5632]]]
    qq = ss_b[:, 0:4096]
    dd_a = [ww[:, 0:384], ww[:, 384:768]]
    dd_b = [ww[:, 768:1152], ww[:, 1152:1536]]
    sq_sl = [ss_b[:, 5632:6144], ss_b[:, 2048:2560]]

    for h in range(2):
        rh = 64 * h
        if h == 1:
            P.wait("sync", "vv", P.now("vv"))
            dsh = dma("sync", sm[64:128, :], sm[0:64, :], "ms")
            P.wait("vector", "ms", dsh)
        for k in range(2):
            tau = 2 * h + k
            slot = tau % 2
            c1 = vvw(lambda e, r=rh, s=slot: e.tensor_copy(
                s2t[r:r + 64, 2048 * s:2048 * s + 2048], sm[r:r + 64, :]))
            P.wait("sync", "vv", c1)
            oth = 64 - rh
            dma("sync", s2t[oth:oth + 64, 2048 * slot:2048 * slot + 2048],
                s2t[rh:rh + 64, 2048 * slot:2048 * slot + 2048], "ms")
            vvw(lambda e, r=rh: e.tensor_tensor(
                sm[r:r + 64, :], sm[r:r + 64, :], bdr_s[r:r + 64, :],
                ALU.mult))
            vvw(lambda e, r=rh, tau=tau: e.tensor_add(
                sm[r:r + 64, :], sm[r:r + 64, :],
                cs[r:r + 64, 2048 * (tau % 2):2048 * (tau % 2) + 2048]))
        P.wait("tensor", "ms", P.now("ms"))
        P.wait("tensor", "vv", P.now("vv"))

        for k in range(2):
            tau = 2 * h + k
            slot = tau % 2
            P.wait("sync", "mm", P.now("mm"))
            vload = {}
            for jt in range(2):
                tt = 2 * tau + jt
                vload[jt] = dma("sync", vsl[jt].rearrange(
                                    "p (w j) -> p w j", w=8),
                                vp[tt].rearrange("w p j -> p w j"),
                                ["vl0", "vl1"][jt])
            P.wait("vector", "mm", P.now("mm"))
            q_done = {}
            for c in range(NCH):
                q_done[c] = vv(lambda e, c=c, tau=tau: e.tensor_tensor(
                    qq[:, 256 * c:256 * c + 256],
                    xa_b[:, 1024 * c + 256 * tau:1024 * c + 256 * tau + 256],
                    qdr[:, 256 * c:256 * c + 256], ALU.mult))
            P.wait("tensor", "vl0", P.now("vl0"))
            P.wait("tensor", "vl1", P.now("vl1"))
            ev_prev = {}
            m_aw_of = {}
            aw_ev_of = {}

            def stage_a(hp):
                if hp >= 2:
                    P.wait("sync", "vv", aw_ev_of.get(hp - 2, 0))
                dd_d = dma("sync", dd_a[hp % 2], ddi[2 * hp],
                           ["d0", "d1"][hp % 2])
                dd_d2 = dma("sync", dd_b[hp % 2], ddi[2 * hp + 1],
                            ["d0", "d1"][hp % 2])
                bset = 2 * (hp % 2)
                P.wait("tensor", "vv", q_done[hp])
                if hp >= 2:
                    P.wait("tensor", "vv", aw_ev_of.get(hp - 2, 0))
                for p in range(2):
                    lo = 64 * p
                    tp = (64, 0) if p else None
                    awb = bank(bset + p)
                    mm(awb[:, 0:256],
                       kva[lo:lo + 64, 1024 * hp + 256 * tau:
                           1024 * hp + 256 * tau + 128],
                       xa_b[lo:lo + 64, 1024 * hp + 256 * tau:
                            1024 * hp + 256 * tau + 256], True, False, tp=tp)
                    mm(awb[:, 256:384],
                       kva[lo:lo + 64, 1024 * hp + 256 * tau + 128:
                           1024 * hp + 256 * tau + 256],
                       xa_b[lo:lo + 64, 1024 * hp + 256 * tau + 128:
                            1024 * hp + 256 * tau + 256], False, True, tp=tp)
                m_aw_of[hp] = P.now("mm")
                return dd_d2

            def stage_b(hp, dd_d2):
                bset = 2 * (hp % 2)
                pcb = 4 + (hp % 2)
                P.wait("vector", "mm", m_aw_of[hp])
                P.wait("vector", ["d0", "d1"][hp % 2], dd_d2)
                if hp >= 2:
                    P.wait("tensor", "vv", ev_prev.get(hp - 2, 0))
                for p in range(2):
                    b = bset + p
                    ddv = dd_a[hp % 2] if p == 0 else dd_b[hp % 2]
                    awt = awt_p[hp % 2][p]
                    v_aw = vv(lambda e, b=b, d=ddv, a=awt: e.tensor_tensor(
                        a, ps[:, 512 * b:512 * b + 384], d[:, 0:384],
                        ALU.mult))
                    if p == 1:
                        aw_ev_of[hp] = v_aw
                    P.wait("tensor", "vv", v_aw)
                    hd = 2 * hp + p
                    lo = 64 * p
                    tpc = (0, 64) if p else None
                    mm(ps[lo:lo + 64, 512 * pcb:512 * pcb + 256],
                       vsl[0][:, 64 * hd:64 * hd + 64], awt[:, 0:256],
                       True, False, tp=tpc)
                    mm(ps[lo:lo + 64, 512 * pcb + 128:512 * pcb + 256],
                       vsl[1][:, 64 * hd:64 * hd + 64], awt[:, 256:384],
                       False, False, tp=tpc)
                    tpi = (lo, lo) if p else None
                    mm(ps[lo:lo + 64, 512 * pcb:512 * pcb + 256],
                       s2t[lo:lo + 64, 2048 * slot + 64 * hd:
                           2048 * slot + 64 * hd + 64],
                       qq[lo:lo + 64, 256 * hp:256 * hp + 256],
                       False, True, tp=tpi)
                m_pc = P.now("mm")
                P.wait("vector", "mm", m_pc)
                ev_prev[hp] = vv(lambda e, b=pcb, hp=hp, h=h, k=k:
                                 e.tensor_copy(attnv(h, hp, 256 * k, 256),
                                               ps[:, 512 * b:512 * b + 256]))

            dd_ctx = {}
            dd_ctx[0] = stage_a(0)
            for hp in range(1, NCH):
                dd_ctx[hp] = stage_a(hp)
                stage_b(hp - 1, dd_ctx[hp - 1])
            stage_b(NCH - 1, dd_ctx[NCH - 1])
        P.wait("scalar", "vv", P.now("vv"))
        for c in range(NCH):
            if c >= 2:
                P.wait("scalar", "mm", mm_r[c - 2])
            s_q = sc(lambda e, c=c, h=h, s=sq_sl[c % 2]: e.activation(
                s, attnv(h, c, 0, 512), AF.Square))
            P.wait("tensor", "sc", s_q)
            mm(ps[0:1, 3072:3584], onescb[:, 0:1], sq_sl[c % 2],
               c == 0, c == NCH - 1)
            if c == 0:
                mm_r = {}
            mm_r[c] = P.now("mm")
        P.wait("vector", "mm", P.now("mm"))
        v_ms = vvw(lambda e: e.tensor_scalar(
            ss[0:1, 1024:1536], ps[0:1, 3072:3584], 1.0 / 2048.0, 1e-5,
            ALU.mult, ALU.add))
        P.wait("scalar", "vv", v_ms)
        s_r = scw(lambda e: e.activation(
            ss[0:1, 1536:2048], ss[0:1, 1024:1536], AF.Sqrt))
        P.wait("vector", "sc", s_r)
        v_rs = vvw(lambda e: e.reciprocal(arow[0:1, 0:512], ss[0:1, 1536:2048]))
        P.wait("tensor", "vv", v_rs)
        mm(bank(7), onesrf[0:1, :], arow[0:1, 0:512], True, True)
        P.wait("vector", "mm", P.now("mm"))
        for c in range(NCH):
            vvw(lambda e, c=c, h=h, t=ss[:, 0:512]: e.tensor_tensor(
                t, attnv(h, c, 0, 512), gav(c, TH * h, TH), ALU.mult))
            vv(lambda e, c=c, h=h, t=ss[:, 0:512]: e.scalar_tensor_tensor(
                gav(c, TH * h, TH), t, nwc[:, c:c + 1], bank(7),
                ALU.mult, ALU.mult))
        P.barrier()

    checkpoint("s6", ga, cast=True)
    def x2v(c, th):
        base = kva_f if th == 0 else xbf_f
        return base[:, 512 * c:512 * c + 512]

    o_rv = {}

    def oproj_pre(w):
        P.wait("sync", "vv", o_rv.get(("tt", w - 1), 0))
        for oc in range(2):
            for th in range(2):
                oca = 2 * w + oc
                o_rv[(w, oc, th)] = dma(
                    "sync", x0sl[2 * oc + th],
                    x1p[128 * oca:128 * oca + 128,
                        512 * th:512 * th + 512], ["r0", "r1"][w % 2])

    def oproj_evac(w, bset, m_w):
        P.wait("vector", "mm", m_w)
        P.wait("vector", ["r0", "r1"][w % 2], o_rv[(w, 1, 1)])
        last = 0
        for oc in range(2):
            for th in range(2):
                oca = 2 * w + oc
                last = vv(lambda e, b=bset + 2 * oc + th,
                          s=x0sl[2 * oc + th], d=x2v(oca, th):
                          e.tensor_tensor(
                              d, ps[:, 512 * b:512 * b + 512], s, ALU.add))
        o_rv[("tt", w)] = last
        return ("vv", last)

    win_phase(cast_fill(wo), lambda ic, th: gav(ic, 512 * th, 512), NCH,
              oproj_evac, pre=oproj_pre)
    P.barrier()

    checkpoint("s7")
    ln_stats_apply(lambda h, c: x2v(c, h),
                   lambda h, c: xa_b[:, 1024 * c + TH * h:
                                     1024 * c + TH * h + TH])
    P.barrier()

    x3sl = [ss[:, 512 * i:512 * i + 512] for i in range(4)]
    OQ = ["scalar", "sync", "scalar", "sync"]
    OS = ["o0", "o1", "o0", "o1"]
    out_dma_cnt = {}



    def ffn2_evac(w, bset, m_w):
        P.wait("vector", "mm", m_w)
        last = 0
        for oc in range(2):
            for th in range(2):
                oca = 2 * w + oc
                sl = 2 * oc + th
                osem = OS[sl]
                if out_dma_cnt.get(sl):
                    P.wait("vector", osem, out_dma_cnt[sl])
                last = vv(lambda e, b=bset + 2 * oc + th, oca=oca,
                          s=x2v(oca, th), o=x3sl[sl]:
                          e.scalar_tensor_tensor(
                              o, ps[:, 512 * b:512 * b + 512],
                              b05c[:, oca:oca + 1], s, ALU.add, ALU.add))
                P.wait(OQ[sl], "vv", last)
                out_dma_cnt[sl] = P.tag(
                    OQ[sl], lambda e, oca=oca, th=th, o=x3sl[sl]:
                    e.dma_start(out=yT[128 * oca:128 * oca + 128,
                                       512 * th:512 * th + 512], in_=o),
                    osem, 16)
        return ("vv", last)

    win_phase(wt_fill_q("gpsimd", "wt2"),
              lambda ic, th: xa_b[:, 1024 * ic + 512 * th:
                                  1024 * ic + 512 * th + 512],
              NCH, ffn2_evac, bias_on=True)
    P.wait("scalar", "o0", P.now("o0"))
    P.wait("scalar", "o1", P.now("o1"))

    _finish(P, nc, st)
    return nc


def _finish(P, nc, st):
    with st:
        with nc.Block() as block:
            @block.tensor
            def _(t):
                for f in P.streams["tensor"]:
                    f(t)

            @block.vector
            def _(v):
                for f in P.streams["vector"]:
                    f(v)

            @block.scalar
            def _(s):
                for f in P.streams["scalar"]:
                    f(s)

            @block.gpsimd
            def _(g):
                for f in P.streams["gpsimd"]:
                    f(g)

            @block.sync
            def _(sy):
                for f in P.streams["sync"]:
                    f(sy)


_NC_CACHE = None


def _tile_w(wT):
    return np.ascontiguousarray(
        wT.reshape(NCH, 128, 8, 256).transpose(2, 1, 0, 3).reshape(
            8, 128, 4096))


def host_inputs(x, qkv_w, out_w, gate_w, norm_w, ffn_w, bias):
    qkv3 = np.asarray(qkv_w).reshape(H, 3, D, DM)
    wq_h = _tile_w(qkv3[:, 0].reshape(DM, DM).T)
    wk_h = _tile_w(qkv3[:, 1].reshape(DM, DM).T)
    wv_h = _tile_w(qkv3[:, 2].reshape(DM, DM).T)
    wg_h = _tile_w(np.asarray(gate_w).T)
    wo_h = _tile_w(np.asarray(out_w).T)
    wf_h = _tile_w(np.asarray(ffn_w).T)
    biasr_h = np.ascontiguousarray(np.asarray(bias).reshape(NCH, 128).T)
    nwc_h = np.ascontiguousarray(np.asarray(norm_w).reshape(NCH, 128).T)
    ddt, qdt, kdc, bdr, psrs = host_constants()
    ident = np.eye(128, dtype=BFNP)
    in_maps = []
    for c in range(8):
        b, t0 = c // 4, (c % 4) * T
        xT_h = np.ascontiguousarray(np.asarray(x)[b, t0:t0 + T, :].T)
        in_maps.append({
            "xT": xT_h, "wq": wq_h, "wk": wk_h, "wv": wv_h, "wg": wg_h,
            "wo": wo_h, "wf": np.ascontiguousarray(wf_h[c:c + 1]),
            "biasr": biasr_h, "nwi": nwc_h,
            "ddi": ddt, "qdi": qdt, "kdi": kdc, "bdi": bdr,
            "psi": psrs[c], "idi": ident,
        })
    return in_maps


def kernel(L0_attn_hidden_states, L0_attn_qkv_w, L0_attn_out_w,
           L0_attn_gate_w, L0_attn_norm_w, L0_ffn_weight, L0_ffn_bias):
    global _NC_CACHE
    if _NC_CACHE is None:
        _NC_CACHE = build_program()
    nc = _NC_CACHE
    in_maps = host_inputs(L0_attn_hidden_states, L0_attn_qkv_w,
                          L0_attn_out_w, L0_attn_gate_w, L0_attn_norm_w,
                          L0_ffn_weight, L0_ffn_bias)
    res = run_bass_kernel_spmd(nc, in_maps, core_ids=list(range(8)))
    out = np.zeros((2, 4096, DM), np.float32)
    for c in range(8):
        b, t0 = c // 4, (c % 4) * T
        out[b, t0:t0 + T, :] = np.asarray(res.results[c]["yT"]).T
    return out




# version 13
# speedup vs baseline: 1.1130x; 1.0493x over previous
import numpy as np
import ml_dtypes
from contextlib import ExitStack

import concourse.bass as bass
import concourse.mybir as mybir
from concourse.bass_utils import run_bass_kernel_spmd

F32, BF16 = mybir.dt.float32, mybir.dt.bfloat16
AF, ALU = mybir.ActivationFunctionType, mybir.AluOpType
AX = mybir.AxisListType.X
BFNP = ml_dtypes.bfloat16

H, D, BS, DM = 32, 64, 256, 2048
T, TH, NB = 1024, 512, 4
NCH = 16

ENGS = ("tensor", "vector", "scalar", "gpsimd", "sync")
SEMS = ["sx", "sw", "gd", "t0", "t1", "r0", "r1", "pk0", "pk1", "q0", "q1",
        "d0", "d1", "vl0", "vl1", "ms", "mq", "o0", "o1", "wt",
        "mm", "vv", "sc", "gp", "cc", "bar", "wt2", "pg0", "pg1",
        "cs0", "cw0"]


def host_constants():
    base = np.float32(1.0) / np.float32(2.0) ** np.float32(8.0 / H)
    rate = base ** (np.arange(H, dtype=np.float32) + 1.0)
    factor = np.float32(1.0 - 0.0 / (32 - 1 + 1e-05) + 1e-05)
    slope = (rate * factor).astype(np.float32)
    r = np.arange(BS, dtype=np.float32) + 1.0
    qd = np.exp(-slope[:, None] * r[None, :]).astype(np.float32)
    kd = np.exp(-slope[:, None] * (BS - r[None, :])).astype(np.float32)
    arg = slope[:, None, None] * (r[:, None] - r[None, :])[None]
    dd = np.where(arg >= 0, np.exp(-np.abs(arg)), 0.0).astype(np.float32)
    bd = np.exp(-slope * np.float32(BS)).astype(np.float32)

    ddt = np.zeros((H, 128, 384), np.float32)
    for h in range(H):
        dT = dd[h].T
        ddt[h, :, 0:256] = dT[0:128, :]
        ddt[h, :, 256:384] = dT[128:256, 128:256]
    qdt = np.zeros((NCH, 128, 256), np.float32)
    for c in range(NCH):
        for p in range(128):
            qdt[c, p, :] = qd[2 * c + p // 64]
    kdc = np.zeros((128, 64), np.float32)
    for h in range(H):
        for jt in range(2):
            kdc[:, 2 * h + jt] = kd[h, 128 * jt:128 * jt + 128]
    bdr = np.tile(np.repeat(bd, D)[None, :], (128, 1)).astype(np.float32)
    psrs = []
    for c in range(8):
        psm = np.zeros((8, 2048), np.float32)
        g0 = (c // 4) * 4
        for j in range(g0, c):
            psm[j] = np.repeat(bd ** (4 * (c - 1 - j)), D)
        psrs.append(psm)
    return ddt.astype(BFNP), qdt.astype(BFNP), kdc, \
        bdr.astype(BFNP), psrs


class Prog:
    def __init__(self):
        self.streams = {k: [] for k in ENGS}
        self.cnt = {}
        self.sems = None
        self.done = False

    def raw(self, eng, fn):
        if self.done:
            return
        self.streams[eng].append(fn)

    def tag(self, eng, fn, sem, n=1):
        if self.done:
            return self.cnt.get(sem, 0)
        self.cnt[sem] = self.cnt.get(sem, 0) + n
        tgt = self.cnt[sem]
        P = self

        def wrapped(e, fn=fn, sem=sem, n=n):
            fn(e).then_inc(P.sems[sem], n)
        self.streams[eng].append(wrapped)
        return tgt

    def wait(self, eng, sem, val):
        if self.done or val is None or val <= 0:
            return
        P = self

        def w(e, sem=sem, val=val):
            e.wait_ge(P.sems[sem], val)
        self.streams[eng].append(w)

    def now(self, sem):
        return self.cnt.get(sem, 0)

    def barrier(self):
        base = self.cnt.get("bar", 0)
        n = len(ENGS)
        for eng in ENGS:
            self.tag(eng, lambda e: e.drain(), "bar")
        for eng in ENGS:
            self.wait(eng, "bar", base + n)


def build_program(stage=None):
    nc = bass.Bass("TRN2", target_bir_lowering=False,
                   detect_race_conditions=False)
    dt = nc.dram_tensor
    xT = dt("xT", [DM, T], F32, kind="ExternalInput").ap()
    wq = dt("wq", [8, 128, 4096], F32, kind="ExternalInput").ap()
    wk = dt("wk", [8, 128, 4096], F32, kind="ExternalInput").ap()
    wv = dt("wv", [8, 128, 4096], F32, kind="ExternalInput").ap()
    wg = dt("wg", [8, 128, 4096], F32, kind="ExternalInput").ap()
    wo = dt("wo", [8, 128, 4096], F32, kind="ExternalInput").ap()
    wf = dt("wf", [1, 128, 4096], F32, kind="ExternalInput").ap()
    biasr = dt("biasr", [128, NCH], F32, kind="ExternalInput").ap()
    nwi = dt("nwi", [128, NCH], F32, kind="ExternalInput").ap()
    ddi = dt("ddi", [H, 128, 384], BF16, kind="ExternalInput").ap()
    qdi = dt("qdi", [NCH, 128, 256], BF16, kind="ExternalInput").ap()
    kdi = dt("kdi", [128, 64], F32, kind="ExternalInput").ap()
    bdi = dt("bdi", [128, DM], BF16, kind="ExternalInput").ap()
    psi = dt("psi", [8, DM], F32, kind="ExternalInput").ap()
    idi = dt("idi", [128, 128], BF16, kind="ExternalInput").ap()
    yT = dt("yT", [DM, T], F32, kind="ExternalOutput").ap()
    x1p = dt("x1p", [DM, T], F32).ap()
    vp = dt("vp", [8, 8, 128, 256], BF16).ap()
    dsc = dt("dsc", [128, 1], F32).ap()
    cci = dt("cci", [64, DM], F32)
    cco = dt("cco", [512, DM], F32)
    ccs_i = dt("ccs_i", [1, 1], F32)
    ccs_o = dt("ccs_o", [1, 1], F32)
    ccw_i = dt("ccw_i", [128, 4096], BF16)
    wtg_t = dt("wtg", [1024, 4096], BF16)
    wtg = wtg_t.ap()
    cw_hold = []

    P = Prog()
    st = ExitStack()
    e = st.enter_context
    xa_f = e(nc.sbuf_tensor("xa", [128, 8192], F32))
    xa_b = xa_f.ap().bitcast(BF16)
    kva_t = e(nc.sbuf_tensor("kva", [128, 16384], BF16))
    kva = kva_t.ap()
    kva_f = kva.bitcast(F32)
    ga_t = e(nc.sbuf_tensor("ga", [128, 16384], BF16))
    ga = ga_t.ap()
    xbf_t = e(nc.sbuf_tensor("xbf", [128, 16384], BF16))
    xbf = xbf_t.ap()
    xbf_f = xbf.bitcast(F32)
    va = e(nc.sbuf_tensor("va", [128, 6144], BF16))
    ww = e(nc.sbuf_tensor("ww", [128, 4096], BF16))
    ss_t = e(nc.sbuf_tensor("ss", [128, 3072], F32))
    ss = ss_t.ap()
    ss_b = ss.bitcast(BF16)
    s2t = e(nc.sbuf_tensor("s2t", [128, 4096], BF16))
    cs = e(nc.sbuf_tensor("cs", [128, 4096], BF16))
    sm = e(nc.sbuf_tensor("sm", [128, DM], F32))
    bdr_s = e(nc.sbuf_tensor("bdr_s", [128, DM], BF16))
    qdr = e(nc.sbuf_tensor("qdr", [128, 4096], BF16))
    kdc_s = e(nc.sbuf_tensor("kdc_s", [128, 64], F32))
    idb = e(nc.sbuf_tensor("idb", [128, 128], BF16))
    nwc = e(nc.sbuf_tensor("nwc", [128, NCH], F32))
    acc = e(nc.sbuf_tensor("acc", [128, 2], F32))
    thrc = e(nc.sbuf_tensor("thrc", [128, 1], F32))
    onescf = e(nc.sbuf_tensor("onescf", [128, 1], F32))
    onescb = e(nc.sbuf_tensor("onescb", [128, 1], BF16))
    onesrf = e(nc.sbuf_tensor("onesrf", [1, 128], F32))
    onesrb = e(nc.sbuf_tensor("onesrb", [1, 512], BF16))
    b05c = e(nc.sbuf_tensor("b05c", [128, NCH], F32))
    arow = e(nc.sbuf_tensor("arow", [1, 1024], F32))
    brow = e(nc.sbuf_tensor("brow", [1, 1024], F32))
    s05v = e(nc.sbuf_tensor("s05v", [1, 1], F32))
    s05c = e(nc.sbuf_tensor("s05c", [128, 1], F32))
    ps_t = e(nc.psum_tensor("ps", [128, 4096], F32))
    ps = ps_t.ap()
    psb = ps.bitcast(BF16)
    sems = {n: e(nc.semaphore(n)) for n in SEMS}
    P.sems = sems

    def bank(i, w=512):
        return ps[:, 512 * i: 512 * i + w]

    def checkpoint(name, dump_ap=None, cast=False):
        if stage == name:
            P.barrier()
            if dump_ap is not None:
                if cast:
                    raw = dump_ap.bitcast(F32) if dump_ap.dtype == BF16 else dump_ap
                    n = dma("sync",
                            yT[0:1024, :].rearrange("(a p) t -> p a t", p=128),
                            raw.rearrange("p (a t) -> p a t", a=8), "o0")
                    n = dma("scalar",
                            yT[1024:2048, :].rearrange("(a p) t -> p a t",
                                                       p=128),
                            raw.rearrange("p (a t) -> p a t", a=8), "o0")
                else:
                    n = dma("sync", yT.rearrange("(a p) t -> p a t", p=128),
                            dump_ap, "o0")
                for eng in ENGS:
                    P.wait(eng, "o0", n)
            P.done = True

    def dma(q, out, in_, sem):
        return P.tag(q, lambda e, o=out, i=in_: e.dma_start(out=o, in_=i),
                     sem, 16)

    def mm(out, lhsT, rhs, start, stop, tp=None):
        return P.tag("tensor",
                     lambda e, o=out, l=lhsT, r=rhs, s=start, p=stop, t=tp:
                     e.matmul(o, l, r, start=s, stop=p, tile_position=t),
                     "mm")

    def vv(fn):
        return P.tag("vector", fn, "vv")

    def vvw(fn):
        t = P.tag("vector", fn, "vv")
        P.wait("vector", "vv", t)
        return t

    def sc(fn):
        return P.tag("scalar", fn, "sc")

    def scw(fn):
        t = P.tag("scalar", fn, "sc")
        P.wait("scalar", "sc", t)
        return t

    def xav(c):
        return xa_f[:, 512 * c: 512 * c + 512]

    def qv(c, t0, n):
        return xa_b[:, 1024 * c + t0: 1024 * c + t0 + n]

    def kv(c, t0, n):
        return kva[:, 1024 * c + t0: 1024 * c + t0 + n]

    def gav(c, t0, n):
        return ga[:, 1024 * c + t0: 1024 * c + t0 + n]

    def xbfv(c, t0, n):
        return xbf[:, 1024 * c + t0: 1024 * c + t0 + n]

    def attnv(h, c, i0, n):
        return xbf[:, 8192 * h + 512 * c + i0: 8192 * h + 512 * c + i0 + n]

    WBUFS = None

    def wwv(w, ic):
        buf = ww if w % 2 == 0 else qdr.ap()
        return buf[:, 256 * ic: 256 * ic + 256]

    dma("sync", bdr_s.ap(), bdi, "sx")
    dma("sync", kdc_s.ap(), kdi, "sx")
    dma("sync", idb.ap(), idi, "sx")
    dma("sync", nwc.ap(), nwi, "sx")
    c_s0 = dma("sync", ss[:, 2080:2080 + NCH], biasr, "ms")
    P.raw("gpsimd", lambda e: e.memset(onescf.ap(), 1.0))
    P.raw("gpsimd", lambda e: e.memset(onescb.ap(), 1.0))
    P.raw("gpsimd", lambda e: e.memset(onesrf.ap(), 1.0))
    P.raw("gpsimd", lambda e: e.memset(onesrb.ap(), 1.0))
    P.tag("gpsimd", lambda e: e.memset(sm.ap(), 0.0), "gp")
    P.wait("vector", "ms", c_s0)
    P.wait("vector", "gp", P.now("gp"))
    vv_b05 = vvw(lambda e: e.tensor_scalar(
        b05c.ap(), ss[:, 2080:2080 + NCH], 0.5, None, ALU.mult))
    v_ms0 = vv(lambda e: e.memset(acc.ap(), 0.0))
    P.wait("vector", "vv", v_ms0)

    kva_f2 = kva.bitcast(F32)
    SLAB = kva_f2[:, 0:4096]
    va_f = va.ap().bitcast(F32)
    d = dma("gpsimd", SLAB, wf[0], "t0")
    P.wait("vector", "t0", d)
    v_r = vv(lambda e: e.reduce_sum(acc[:, 0:1], SLAB, axis=AX,
                                    apply_absolute_value=True))
    P.wait("sync", "vv", v_r)
    d_a1 = dma("sync", dsc, acc[:, 0:1], "ms")
    P.wait("sync", "ms", d_a1)
    d_ac = dma("sync", va_f[0:1, 0:128], dsc.rearrange("p o -> o p"), "ms")
    P.wait("vector", "ms", d_ac)
    v_p = vvw(lambda e: e.reduce_sum(va_f[0:1, 128:129], va_f[0:1, 0:128],
                                     axis=AX))
    P.wait("gpsimd", "vv", v_p)
    d_ci0 = dma("gpsimd", ccs_i.ap(), va_f[0:1, 128:129], "gd")
    P.wait("gpsimd", "gd", d_ci0)
    cs0_t = P.tag("gpsimd", lambda e: e.collective_compute(
        "AllReduce", ALU.add, replica_groups=[list(range(8))],
        ins=[ccs_i.ap()], outs=[ccs_o.ap()]), "cs0")
    P.wait("tensor", "gp", P.now("gp"))

    gaf = ga.bitcast(F32)

    def ln_stats_apply(src_view, dst_view, mid_hook=None):
        cast_sl = [ss_b[:, 4096 + 512 * i:4608 + 512 * i] for i in range(4)]
        sq_sl = [cs[:, 512 * i:512 * i + 512] for i in range(4)]
        sumc = {0: 3072, 1: 2048}
        sqc = {0: 3584, 1: 2560}
        mm_ch = {}
        for i in range(2 * NCH):
            h, c = i // NCH, i % NCH
            if i >= 4:
                P.wait("scalar", "mm", mm_ch[i - 4])
                P.wait("vector", "mm", mm_ch[i - 4])
            s_c = sc(lambda e, h=h, c=c, s=cast_sl[i % 4]:
                     e.copy(s, src_view(h, c)))
            s_q = vv(lambda e, h=h, c=c, q=sq_sl[i % 4]:
                     e.tensor_tensor(q, src_view(h, c), src_view(h, c),
                                     ALU.mult))
            P.wait("tensor", "sc", s_c)
            mm(ps[0:1, sumc[h]:sumc[h] + 512], onescb[:, 0:1],
               cast_sl[i % 4], c == 0, c == NCH - 1)
            P.wait("tensor", "vv", s_q)
            mm(ps[0:1, sqc[h]:sqc[h] + 512], onescb[:, 0:1],
               sq_sl[i % 4], c == 0, c == NCH - 1)
            mm_ch[i] = P.now("mm")
        if mid_hook is not None:
            mid_hook()
        P.wait("vector", "mm", P.now("mm"))
        mrow = ss[0:1, 1024:2048]
        qrow = ss[0:1, 2048:3072]
        trow = ss[0:1, 0:1024]
        vvw(lambda e: e.tensor_scalar(ss[0:1, 1024:1536], ps[0:1, 3072:3584],
                                      1.0 / 2048.0, None, ALU.mult))
        vvw(lambda e: e.tensor_scalar(ss[0:1, 1536:2048], ps[0:1, 2048:2560],
                                      1.0 / 2048.0, None, ALU.mult))
        vvw(lambda e: e.tensor_scalar(ss[0:1, 2048:2560], ps[0:1, 3584:4096],
                                      1.0 / 2048.0, None, ALU.mult))
        vvw(lambda e: e.tensor_scalar(ss[0:1, 2560:3072], ps[0:1, 2560:3072],
                                      1.0 / 2048.0, None, ALU.mult))
        vvw(lambda e: e.tensor_tensor(trow, mrow, mrow, ALU.mult))
        vvw(lambda e: e.scalar_tensor_tensor(
            trow, trow, -1.0, qrow, ALU.mult, ALU.add))
        v_var = vvw(lambda e: e.tensor_scalar(trow, trow, 1e-5, None,
                                              ALU.add))
        P.wait("scalar", "vv", v_var)
        s_sq = scw(lambda e: e.activation(qrow, trow, AF.Sqrt))
        P.wait("vector", "sc", s_sq)
        vvw(lambda e: e.reciprocal(arow[0:1, 0:1024], qrow))
        v_ab = vvw(lambda e: e.scalar_tensor_tensor(
            brow[0:1, 0:1024], mrow, -1.0, arow[0:1, 0:1024],
            ALU.mult, ALU.mult))
        P.wait("tensor", "vv", v_ab)
        AB = {0: (4, 5), 1: (2, 3)}
        mm(bank(4), onesrf[0:1, :], arow[0:1, 0:512], True, True)
        mm(bank(5), onesrf[0:1, :], brow[0:1, 0:512], True, True)
        mm(bank(2), onesrf[0:1, :], arow[0:1, 512:1024], True, True)
        mm(bank(3), onesrf[0:1, :], brow[0:1, 512:1024], True, True)
        P.wait("vector", "mm", P.now("mm"))
        tmp_sl = [ss[:, 0:512], ss[:, 512:1024]]
        for i in range(2 * NCH):
            h, c = i // NCH, i % NCH
            vvw(lambda e, h=h, c=c, t=tmp_sl[i % 2], a=AB[h][0]:
                e.tensor_tensor(t, src_view(h, c), bank(a), ALU.mult))
            vv(lambda e, h=h, c=c, t=tmp_sl[i % 2], b=AB[h][1]:
               e.tensor_tensor(dst_view(h, c), t, bank(b), ALU.add))
        return P.now("vv")

    checkpoint("s0")
    tern_tn = []

    def emit_tern_body():
        P.wait("sync", "cs0", cs0_t)
        d_so = dma("sync", va_f[0:1, 128:129], ccs_o.ap(), "ms")
        P.wait("vector", "ms", d_so)
        vvw(lambda e: e.tensor_scalar(
            s05v[0:1, 0:1], va_f[0:1, 128:129], 0.5 / 4194304.0, None, ALU.mult))
        v_t0 = vvw(lambda e: e.tensor_scalar(
            va_f[0:1, 129:130], va_f[0:1, 128:129], 0.5 / 4194304.0, 5e-11,
            ALU.mult, ALU.add))
        v_t0b = vvw(lambda e: e.tensor_copy(thrc[0:1, 0:1], va_f[0:1, 129:130]))
        v_t0c = vvw(lambda e: e.tensor_copy(s05c[0:1, 0:1], s05v[0:1, 0:1]))
        P.wait("sync", "vv", v_t0c)
        k = 1
        while k < 128:
            dma("sync", thrc[k:2 * k, 0:1], thrc[0:k, 0:1], "ms")
            dma("sync", s05c[k:2 * k, 0:1], s05c[0:k, 0:1], "ms")
            P.wait("sync", "ms", P.now("ms"))
            k *= 2
        P.wait("vector", "ms", P.now("ms"))
        vvw(lambda e: e.tensor_scalar(acc[:, 1:2], thrc[:, 0:1], -1.0, None,
                                      ALU.mult))
        vv(lambda e: e.tensor_scalar(
            ss_b[:, 0:4096], SLAB, thrc[:, 0:1], s05c[:, 0:1],
            ALU.is_gt, ALU.mult))
        vvw(lambda e: e.tensor_scalar(
            ga[:, 0:4096], SLAB, acc[:, 1:2], s05c[:, 0:1],
            ALU.is_lt, ALU.mult))
        tn = vv(lambda e: e.tensor_sub(
            ga[:, 4096:8192], ss_b[:, 0:4096], ga[:, 0:4096]))
        P.wait("gpsimd", "vv", tn)
        d_cw = dma("gpsimd", ccw_i.ap(), ga[:, 4096:8192], "wt")
        P.wait("gpsimd", "wt", d_cw)
        cw_hold.append(P.tag("gpsimd", lambda e: e.collective_compute(
            "AllGather", ALU.bypass, replica_groups=[list(range(8))],
            ins=[ccw_i.ap()], outs=[wtg_t.ap()]), "cw0"))

        tern_tn.append(tn)

    def h1v(c):
        base = gaf if c < 8 else kva_f2
        cc = c % 8
        return base[:, 4096 + 512 * cc:4608 + 512 * cc]

    def ln_src(h, c):
        return xav(c) if h == 0 else h1v(c)

    for c in range(NCH):
        dma("sync", xav(c), xT[128 * c:128 * c + 128, 0:TH], "sx")
        dma("scalar", h1v(c), xT[128 * c:128 * c + 128, TH:T], "sx")
    P.wait("scalar", "sx", P.now("sx"))
    P.wait("vector", "sx", P.now("sx"))

    def emit_tern():
        emit_tern_body()

    ln_stats_apply(ln_src, lambda h, c: xbfv(c, TH * h, TH),
                   mid_hook=emit_tern)
    P.barrier()
    checkpoint("s1h0", xbf, cast=True)

    def win_phase(fill, rhs_view, n_ic, evac, bias_on=False, pre=None):
        guards = {}
        mcnt = {}
        for w in range(8):
            rsem, rval = fill(w, mcnt.get(w - 2, 0))
            if pre is not None:
                pre(w)
            P.wait("tensor", rsem, rval)
            if w >= 2:
                gsem, gval = guards[w - 2]
                P.wait("tensor", gsem, gval)
            bset = 4 * (w % 2)
            for ic in range(n_ic):
                for oc in range(2):
                    for th in range(2):
                        mm(bank(bset + 2 * oc + th),
                           wwv(w, ic)[:, 128 * oc:128 * oc + 128],
                           rhs_view(ic, th), ic == 0, ic == n_ic - 1)
            mcnt[w] = P.now("mm")
            guards[w] = evac(w, bset, mcnt[w])
        for w in (6, 7):
            gsem, gval = guards[w]
            P.wait("tensor", gsem, gval)
        return mcnt

    checkpoint("s1", xbf, cast=True)
    x0sl = [ss[:, 512 * i:512 * i + 512] for i in range(4)]
    x1sl = [ss[:, 2048:2560], ss[:, 2560:3072]]

    def wt_fill_q(q, sem):
        def fill(w, prev_mm):
            P.wait(q, "mm", prev_mm)
            if w == 0:
                P.wait(q, "cw0", cw_hold[0])
            buf = ww if w % 2 == 0 else qdr.ap()
            d = dma(q, buf[:, 0:4096], wtg[128 * w:128 * w + 128, :], sem)
            return (sem, d)
        return fill

    def ffn_pre_evac(res_src, out_dst):
        RS = ["r0", "r1"]
        PK = ["pg0", "pg1"]
        rv_of = {}

        def pre(w):
            P.wait("sync", "vv", rv_of.get(("tt", w - 1), 0))
            for oc in range(2):
                for th in range(2):
                    oca = 2 * w + oc
                    rv_of[(w, oc, th)] = dma(
                        "sync", x0sl[2 * oc + th],
                        res_src[128 * oca:128 * oca + 128,
                                512 * th:512 * th + 512], RS[w % 2])

        def evac(w, bset, m_w):
            P.wait("vector", "mm", m_w)
            P.wait("vector", RS[w % 2], rv_of[(w, 1, 1)])
            last = 0
            for oc in range(2):
                for th in range(2):
                    oca = 2 * w + oc
                    sl = th
                    P.wait("vector", PK[sl], P.now(PK[sl]))
                    last = vv(lambda e, b=bset + 2 * oc + th, oca=oca,
                              s=x0sl[2 * oc + th], o=x1sl[sl]:
                              e.scalar_tensor_tensor(
                                  o, ps[:, 512 * b:512 * b + 512],
                                  b05c[:, oca:oca + 1], s, ALU.add,
                                  ALU.add))
                    P.wait("gpsimd", "vv", last)
                    dma("gpsimd",
                        out_dst[128 * oca:128 * oca + 128,
                                512 * th:512 * th + 512], x1sl[sl], PK[sl])
            rv_of[("tt", w)] = last
            return ("vv", last)
        return pre, evac

    f1pre, f1evac = ffn_pre_evac(xT, x1p)
    P.wait("sync", "vv", tern_tn[0])
    win_phase(wt_fill_q("scalar", "sw"), lambda ic, th: xbfv(ic, 512 * th, 512),
              NCH, f1evac, bias_on=True, pre=f1pre)
    P.barrier()

    checkpoint("s2")
    for q in ("sync", "scalar"):
        P.wait(q, "pg0", P.now("pg0"))
        P.wait(q, "pg1", P.now("pg1"))
    for c in range(NCH):
        dma("sync", xav(c), x1p[128 * c:128 * c + 128, 0:TH], "sx")
        dma("scalar", h1v(c), x1p[128 * c:128 * c + 128, TH:T], "sx")
    P.wait("scalar", "sx", P.now("sx"))
    P.wait("vector", "sx", P.now("sx"))
    ln_stats_apply(ln_src, lambda h, c: xbfv(c, TH * h, TH))
    P.barrier()

    checkpoint("s3", xbf, cast=True)
    def cast_fill(wsrc):
        def fill(w, prev_mm):
            P.wait("gpsimd", "mm", prev_mm)
            buf = ww if w % 2 == 0 else qdr.ap()
            d = dma("gpsimd", buf[:, 0:4096], wsrc[w], "gd")
            return ("gd", d)
        return fill

    def silu_evac(dst_view, act):
        def evac(w, bset, m_w):
            P.wait("scalar", "mm", m_w)
            last = 0
            for oc in range(2):
                for th in range(2):
                    last = sc(lambda e, b=bset + 2 * oc + th,
                              d=dst_view(2 * w + oc, th): e.activation(
                                  d, ps[:, 512 * b:512 * b + 512], act))
            return ("sc", last)
        return evac

    win_phase(cast_fill(wk), lambda ic, th: xbfv(ic, 512 * th, 512), NCH,
              silu_evac(lambda oca, th: kv(oca, 512 * th, 512), AF.Silu))
    P.barrier()

    checkpoint("s4a", kva, cast=True)
    kkd_bufs = [ss_b[:, 0:2048], ss_b[:, 2048:4096]]
    vsl = [va[:, 0:2048], va[:, 2048:4096]]
    PKS = ["pk0", "pk1"]
    ev_v = {}
    mv = {}
    pk_tot = {0: 0, 1: 0}
    for w in range(8):
        P.wait("gpsimd", "mm", mv.get(w - 2, 0))
        buf = ww if w % 2 == 0 else qdr.ap()
        d = dma("gpsimd", buf[:, 0:4096], wv[w], "gd")
        P.wait("tensor", "gd", d)
        if w >= 2:
            P.wait("tensor", "sc", ev_v[w - 2])
        bset = 4 * (w % 2)
        for tt in range(8):
            bk, off = bset + tt % 4, 256 * (tt // 4)
            for ic in range(NCH):
                mm(ps[:, 512 * bk + off:512 * bk + off + 256],
                   xbfv(ic, 128 * tt, 128), wwv(w, ic), ic == 0,
                   ic == NCH - 1)
        mv[w] = P.now("mm")
        P.wait("scalar", "mm", mv[w])
        P.wait("scalar", PKS[w % 2], pk_tot[w % 2])
        last = 0
        for tt in range(8):
            bk, off = bset + tt % 4, 256 * (tt // 4)
            scol = 256 * (8 * (w % 2) + tt)
            last = sc(lambda e, b=bk, o=off, sc0=scol: e.activation(
                va[:, sc0:sc0 + 256], ps[:, 512 * b + o:512 * b + o + 256],
                AF.Silu))
            P.wait("sync", "sc", last)
            dma("sync", vp[tt, w], va[:, scol:scol + 256], PKS[w % 2])
        ev_v[w] = last
        pk_tot[w % 2] = P.now(PKS[w % 2])
    P.barrier()

    c_started = set()
    prev_cmm = 0
    for tt in range(8):
        tau, jt = tt // 2, tt % 2
        dstv = kkd_bufs[tt % 2]
        P.wait("sync", "mm", prev_cmm)
        vld = dma("sync", vsl[tt % 2].rearrange("p (w j) -> p w j", w=8),
                  vp[tt].rearrange("w p j -> p w j"),
                  ["vl0", "vl1"][tt % 2])
        kkd_ev = {}
        for c in range(NCH):
            bk = c % 2
            tp_dst = psb[:, 1024 * bk:1024 * bk + 128]
            if c >= 2:
                P.wait("tensor", "vv", kkd_ev[c - 2])
            P.tag("tensor", lambda e, d=tp_dst, c=c, tau=tau, jt=jt:
                  e.matmul(d, kv(c, 256 * tau + 128 * jt, 128), idb.ap(),
                           start=True, stop=True, is_transpose=True), "mm")
            m = P.now("mm")
            P.wait("vector", "mm", m)
            for p in range(2):
                hd = 2 * c + p
                kkd_ev[c] = vv(lambda e, d=tp_dst, c=c, p=p, hd=hd, jt=jt,
                               dv=dstv: e.tensor_scalar(
                    dv[:, 128 * c + 64 * p:128 * c + 64 * p + 64],
                    d[:, 64 * p:64 * p + 64],
                    kdc_s[:, 2 * hd + jt:2 * hd + jt + 1], None, ALU.mult))
        kkd_done = P.now("vv")
        P.wait("tensor", "vv", kkd_done)
        P.wait("tensor", ["vl0", "vl1"][tt % 2], vld)
        rhalf = 64 * (tau // 2)
        for hd in range(32):
            cbank = 4 + (64 * hd) // 512
            key = (cbank, tau)
            first = key not in c_started
            c_started.add(key)
            last_c = (jt == 1) and (hd % 8 == 7)
            tp = (0, 64) if rhalf else None
            mm(ps[rhalf:rhalf + 64, 2048 + 64 * hd:2048 + 64 * hd + 64],
               dstv[:, 128 * (hd // 2) + 64 * (hd % 2):
                    128 * (hd // 2) + 64 * (hd % 2) + 64],
               vsl[tt % 2][:, 64 * hd:64 * hd + 64],
               first, last_c, tp=tp)
        prev_cmm = P.now("mm")
        if jt == 1:
            P.wait("vector", "mm", prev_cmm)
            cv = vv(lambda e, r=rhalf, tau=tau: e.tensor_copy(
                cs[r:r + 64, 2048 * (tau % 2):2048 * (tau % 2) + 2048],
                ps[r:r + 64, 2048:4096]))
            P.wait("tensor", "vv", cv)
    P.barrier()
    checkpoint("s4b")
    dma("sync", ss_b[0:64, 0:2048], cs[64:128, 0:2048], "ms")
    d_c3 = dma("sync", ss_b[0:64, 4096:6144], cs[64:128, 2048:4096], "ms")
    P.wait("vector", "ms", d_c3)
    Pv = ps[0:64, 0:2048]
    vvw(lambda e: e.tensor_copy(Pv, cs[0:64, 0:2048]))
    vvw(lambda e: e.tensor_tensor(Pv, Pv, bdr_s[0:64, :], ALU.mult))
    vvw(lambda e: e.tensor_add(Pv, Pv, cs[0:64, 2048:4096]))
    vvw(lambda e: e.tensor_tensor(Pv, Pv, bdr_s[0:64, :], ALU.mult))
    vvw(lambda e: e.tensor_add(Pv, Pv, ss_b[0:64, 0:2048]))
    vvw(lambda e: e.tensor_tensor(Pv, Pv, bdr_s[0:64, :], ALU.mult))
    v_pub = vvw(lambda e: e.tensor_add(ss[0:64, 0:2048], Pv,
                                       ss_b[0:64, 4096:6144]))
    P.wait("gpsimd", "vv", v_pub)
    d_ci = dma("gpsimd", cci.ap(), ss[0:64, 0:2048], "gd")
    P.wait("gpsimd", "gd", d_ci)
    cc_t = P.tag("gpsimd", lambda e: e.collective_compute(
        "AllGather", ALU.bypass, replica_groups=[list(range(8))],
        ins=[cci.ap()], outs=[cco.ap()]), "cc")
    P.barrier()

    checkpoint("s4c")
    win_phase(cast_fill(wq), lambda ic, th: xbfv(ic, 512 * th, 512), NCH,
              silu_evac(lambda oca, th: qv(oca, 512 * th, 512), AF.Silu))
    P.barrier()
    win_phase(cast_fill(wg), lambda ic, th: xbfv(ic, 512 * th, 512), NCH,
              silu_evac(lambda oca, th: gav(oca, 512 * th, 512), AF.Sigmoid))
    P.barrier()

    checkpoint("s5", xa_b, cast=True)
    P.wait("gpsimd", "cc", cc_t)
    P.barrier()
    dma("sync", qdr.ap().rearrange("p (a t) -> p a t", a=16),
        qdi.rearrange("a p t -> p a t"), "sx")
    P.wait("vector", "sx", P.now("sx"))
    s2t_f = s2t.ap().bitcast(F32)
    ww_f = ww.ap().bitcast(F32)
    psr_sl = [s2t_f[0:1, 0:2048], ww_f[0:1, 0:2048], va_f[0:1, 0:2048]]
    sp_sl = [xbf_f[0:64, 2048 * i:2048 * i + 2048] for i in range(4)]
    m_j, tt_j, add_j = {}, {}, {}
    for j in range(7):
        rs, ss4 = j % 3, j % 4
        if j >= 3:
            P.wait("scalar", "mm", m_j[j - 3])
        if j >= 4:
            P.wait("sync", "vv", tt_j[j - 4])
        dj = dma("scalar", psr_sl[rs], psi[j:j + 1, :], "mq")
        ds = dma("sync", sp_sl[ss4], cco.ap()[64 * j:64 * j + 64, :], "ms")
        base = 2048 * (j % 2)
        P.wait("tensor", "mq", dj)
        if j >= 2:
            P.wait("tensor", "vv", add_j[j - 2])
        for q4 in range(4):
            mm(ps[0:64, base + 512 * q4:base + 512 * q4 + 512],
               onesrf[0:1, 0:64], psr_sl[rs][0:1, 512 * q4:512 * q4 + 512],
               True, True)
        m_j[j] = P.now("mm")
        P.wait("vector", "mm", m_j[j])
        P.wait("vector", "ms", ds)
        tt_j[j] = vvw(lambda e, b=base, t=sp_sl[ss4]: e.tensor_tensor(
            ps[0:64, b:b + 2048], t, ps[0:64, b:b + 2048], ALU.mult))
        add_j[j] = vvw(lambda e, b=base: e.tensor_add(
            sm[0:64, :], sm[0:64, :], ps[0:64, b:b + 2048]))
    if add_j:
        P.wait("tensor", "vv", add_j[6])
        P.wait("sync", "vv", add_j[6])
        P.wait("scalar", "vv", add_j[6])
    P.barrier()

    checkpoint("s6a")
    awt_p = [[ss_b[:, 4096:4480], ss_b[:, 4480:4864]],
             [ss_b[:, 4864:5248], ss_b[:, 5248:5632]]]
    qq = ss_b[:, 0:4096]
    dd_a = [ww[:, 0:384], ww[:, 384:768]]
    dd_b = [ww[:, 768:1152], ww[:, 1152:1536]]
    sq_sl = [ss_b[:, 5632:6144], ss_b[:, 2048:2560]]

    for h in range(2):
        rh = 64 * h
        if h == 1:
            P.wait("sync", "vv", P.now("vv"))
            dsh = dma("sync", sm[64:128, :], sm[0:64, :], "ms")
            P.wait("vector", "ms", dsh)
        for k in range(2):
            tau = 2 * h + k
            slot = tau % 2
            c1 = vvw(lambda e, r=rh, s=slot: e.tensor_copy(
                s2t[r:r + 64, 2048 * s:2048 * s + 2048], sm[r:r + 64, :]))
            P.wait("sync", "vv", c1)
            oth = 64 - rh
            dma("sync", s2t[oth:oth + 64, 2048 * slot:2048 * slot + 2048],
                s2t[rh:rh + 64, 2048 * slot:2048 * slot + 2048], "ms")
            vvw(lambda e, r=rh: e.tensor_tensor(
                sm[r:r + 64, :], sm[r:r + 64, :], bdr_s[r:r + 64, :],
                ALU.mult))
            vvw(lambda e, r=rh, tau=tau: e.tensor_add(
                sm[r:r + 64, :], sm[r:r + 64, :],
                cs[r:r + 64, 2048 * (tau % 2):2048 * (tau % 2) + 2048]))
        P.wait("tensor", "ms", P.now("ms"))
        P.wait("tensor", "vv", P.now("vv"))

        for k in range(2):
            tau = 2 * h + k
            slot = tau % 2
            P.wait("sync", "mm", P.now("mm"))
            vload = {}
            for jt in range(2):
                tt = 2 * tau + jt
                vload[jt] = dma("sync", vsl[jt].rearrange(
                                    "p (w j) -> p w j", w=8),
                                vp[tt].rearrange("w p j -> p w j"),
                                ["vl0", "vl1"][jt])
            P.wait("vector", "mm", P.now("mm"))
            q_done = {}
            for c in range(NCH):
                q_done[c] = vv(lambda e, c=c, tau=tau: e.tensor_tensor(
                    qq[:, 256 * c:256 * c + 256],
                    xa_b[:, 1024 * c + 256 * tau:1024 * c + 256 * tau + 256],
                    qdr[:, 256 * c:256 * c + 256], ALU.mult))
            P.wait("tensor", "vl0", P.now("vl0"))
            P.wait("tensor", "vl1", P.now("vl1"))
            ev_prev = {}
            m_aw_of = {}
            aw_ev_of = {}

            def stage_a(hp):
                if hp >= 2:
                    P.wait("sync", "vv", aw_ev_of.get(hp - 2, 0))
                dd_d = dma("sync", dd_a[hp % 2], ddi[2 * hp],
                           ["d0", "d1"][hp % 2])
                dd_d2 = dma("sync", dd_b[hp % 2], ddi[2 * hp + 1],
                            ["d0", "d1"][hp % 2])
                bset = 2 * (hp % 2)
                P.wait("tensor", "vv", q_done[hp])
                if hp >= 2:
                    P.wait("tensor", "vv", aw_ev_of.get(hp - 2, 0))
                for p in range(2):
                    lo = 64 * p
                    tp = (64, 0) if p else None
                    awb = bank(bset + p)
                    mm(awb[:, 0:256],
                       kva[lo:lo + 64, 1024 * hp + 256 * tau:
                           1024 * hp + 256 * tau + 128],
                       xa_b[lo:lo + 64, 1024 * hp + 256 * tau:
                            1024 * hp + 256 * tau + 256], True, False, tp=tp)
                    mm(awb[:, 256:384],
                       kva[lo:lo + 64, 1024 * hp + 256 * tau + 128:
                           1024 * hp + 256 * tau + 256],
                       xa_b[lo:lo + 64, 1024 * hp + 256 * tau + 128:
                            1024 * hp + 256 * tau + 256], False, True, tp=tp)
                m_aw_of[hp] = P.now("mm")
                return dd_d2

            def stage_b(hp, dd_d2):
                bset = 2 * (hp % 2)
                pcb = 4 + (hp % 2)
                P.wait("vector", "mm", m_aw_of[hp])
                P.wait("vector", ["d0", "d1"][hp % 2], dd_d2)
                if hp >= 2:
                    P.wait("tensor", "vv", ev_prev.get(hp - 2, 0))
                for p in range(2):
                    b = bset + p
                    ddv = dd_a[hp % 2] if p == 0 else dd_b[hp % 2]
                    awt = awt_p[hp % 2][p]
                    v_aw = vv(lambda e, b=b, d=ddv, a=awt: e.tensor_tensor(
                        a, ps[:, 512 * b:512 * b + 384], d[:, 0:384],
                        ALU.mult))
                    if p == 1:
                        aw_ev_of[hp] = v_aw
                    P.wait("tensor", "vv", v_aw)
                    hd = 2 * hp + p
                    lo = 64 * p
                    tpc = (0, 64) if p else None
                    mm(ps[lo:lo + 64, 512 * pcb:512 * pcb + 256],
                       vsl[0][:, 64 * hd:64 * hd + 64], awt[:, 0:256],
                       True, False, tp=tpc)
                    mm(ps[lo:lo + 64, 512 * pcb + 128:512 * pcb + 256],
                       vsl[1][:, 64 * hd:64 * hd + 64], awt[:, 256:384],
                       False, False, tp=tpc)
                    tpi = (lo, lo) if p else None
                    mm(ps[lo:lo + 64, 512 * pcb:512 * pcb + 256],
                       s2t[lo:lo + 64, 2048 * slot + 64 * hd:
                           2048 * slot + 64 * hd + 64],
                       qq[lo:lo + 64, 256 * hp:256 * hp + 256],
                       False, True, tp=tpi)
                m_pc = P.now("mm")
                P.wait("vector", "mm", m_pc)
                ev_prev[hp] = vv(lambda e, b=pcb, hp=hp, h=h, k=k:
                                 e.tensor_copy(attnv(h, hp, 256 * k, 256),
                                               ps[:, 512 * b:512 * b + 256]))

            dd_ctx = {}
            dd_ctx[0] = stage_a(0)
            for hp in range(1, NCH):
                dd_ctx[hp] = stage_a(hp)
                stage_b(hp - 1, dd_ctx[hp - 1])
            stage_b(NCH - 1, dd_ctx[NCH - 1])
        P.wait("scalar", "vv", P.now("vv"))
        for c in range(NCH):
            if c >= 2:
                P.wait("scalar", "mm", mm_r[c - 2])
            s_q = sc(lambda e, c=c, h=h, s=sq_sl[c % 2]: e.activation(
                s, attnv(h, c, 0, 512), AF.Square))
            P.wait("tensor", "sc", s_q)
            mm(ps[0:1, 3072:3584], onescb[:, 0:1], sq_sl[c % 2],
               c == 0, c == NCH - 1)
            if c == 0:
                mm_r = {}
            mm_r[c] = P.now("mm")
        P.wait("vector", "mm", P.now("mm"))
        v_ms = vvw(lambda e: e.tensor_scalar(
            ss[0:1, 1024:1536], ps[0:1, 3072:3584], 1.0 / 2048.0, 1e-5,
            ALU.mult, ALU.add))
        P.wait("scalar", "vv", v_ms)
        s_r = scw(lambda e: e.activation(
            ss[0:1, 1536:2048], ss[0:1, 1024:1536], AF.Sqrt))
        P.wait("vector", "sc", s_r)
        v_rs = vvw(lambda e: e.reciprocal(arow[0:1, 0:512], ss[0:1, 1536:2048]))
        P.wait("tensor", "vv", v_rs)
        mm(bank(7), onesrf[0:1, :], arow[0:1, 0:512], True, True)
        P.wait("vector", "mm", P.now("mm"))
        for c in range(NCH):
            vvw(lambda e, c=c, h=h, t=ss[:, 0:512]: e.tensor_tensor(
                t, attnv(h, c, 0, 512), gav(c, TH * h, TH), ALU.mult))
            vv(lambda e, c=c, h=h, t=ss[:, 0:512]: e.scalar_tensor_tensor(
                gav(c, TH * h, TH), t, nwc[:, c:c + 1], bank(7),
                ALU.mult, ALU.mult))
        P.barrier()

    checkpoint("s6", ga, cast=True)
    def x2v(c, th):
        base = kva_f if th == 0 else xbf_f
        return base[:, 512 * c:512 * c + 512]

    o_rv = {}

    def oproj_pre(w):
        P.wait("sync", "vv", o_rv.get(("tt", w - 1), 0))
        for oc in range(2):
            for th in range(2):
                oca = 2 * w + oc
                o_rv[(w, oc, th)] = dma(
                    "sync", x0sl[2 * oc + th],
                    x1p[128 * oca:128 * oca + 128,
                        512 * th:512 * th + 512], ["r0", "r1"][w % 2])

    def oproj_evac(w, bset, m_w):
        P.wait("vector", "mm", m_w)
        P.wait("vector", ["r0", "r1"][w % 2], o_rv[(w, 1, 1)])
        last = 0
        for oc in range(2):
            for th in range(2):
                oca = 2 * w + oc
                last = vv(lambda e, b=bset + 2 * oc + th,
                          s=x0sl[2 * oc + th], d=x2v(oca, th):
                          e.tensor_tensor(
                              d, ps[:, 512 * b:512 * b + 512], s, ALU.add))
        o_rv[("tt", w)] = last
        return ("vv", last)

    win_phase(cast_fill(wo), lambda ic, th: gav(ic, 512 * th, 512), NCH,
              oproj_evac, pre=oproj_pre)
    P.barrier()

    checkpoint("s7")
    ln_stats_apply(lambda h, c: x2v(c, h),
                   lambda h, c: xa_b[:, 1024 * c + TH * h:
                                     1024 * c + TH * h + TH])
    P.barrier()

    x3sl = [ss[:, 512 * i:512 * i + 512] for i in range(4)]
    OQ = ["scalar", "sync", "scalar", "sync"]
    OS = ["o0", "o1", "o0", "o1"]
    out_dma_cnt = {}



    def ffn2_evac(w, bset, m_w):
        P.wait("vector", "mm", m_w)
        last = 0
        for oc in range(2):
            for th in range(2):
                oca = 2 * w + oc
                sl = 2 * oc + th
                osem = OS[sl]
                if out_dma_cnt.get(sl):
                    P.wait("vector", osem, out_dma_cnt[sl])
                last = vv(lambda e, b=bset + 2 * oc + th, oca=oca,
                          s=x2v(oca, th), o=x3sl[sl]:
                          e.scalar_tensor_tensor(
                              o, ps[:, 512 * b:512 * b + 512],
                              b05c[:, oca:oca + 1], s, ALU.add, ALU.add))
                P.wait(OQ[sl], "vv", last)
                out_dma_cnt[sl] = P.tag(
                    OQ[sl], lambda e, oca=oca, th=th, o=x3sl[sl]:
                    e.dma_start(out=yT[128 * oca:128 * oca + 128,
                                       512 * th:512 * th + 512], in_=o),
                    osem, 16)
        return ("vv", last)

    win_phase(wt_fill_q("gpsimd", "wt2"),
              lambda ic, th: xa_b[:, 1024 * ic + 512 * th:
                                  1024 * ic + 512 * th + 512],
              NCH, ffn2_evac, bias_on=True)
    P.wait("scalar", "o0", P.now("o0"))
    P.wait("scalar", "o1", P.now("o1"))

    _finish(P, nc, st)
    return nc


def _finish(P, nc, st):
    with st:
        with nc.Block() as block:
            @block.tensor
            def _(t):
                for f in P.streams["tensor"]:
                    f(t)

            @block.vector
            def _(v):
                for f in P.streams["vector"]:
                    f(v)

            @block.scalar
            def _(s):
                for f in P.streams["scalar"]:
                    f(s)

            @block.gpsimd
            def _(g):
                for f in P.streams["gpsimd"]:
                    f(g)

            @block.sync
            def _(sy):
                for f in P.streams["sync"]:
                    f(sy)


_NC_CACHE = None


def _tile_w(wT):
    return np.ascontiguousarray(
        wT.reshape(NCH, 128, 8, 256).transpose(2, 1, 0, 3).reshape(
            8, 128, 4096))


def host_inputs(x, qkv_w, out_w, gate_w, norm_w, ffn_w, bias):
    qkv3 = np.asarray(qkv_w).reshape(H, 3, D, DM)
    wq_h = _tile_w(qkv3[:, 0].reshape(DM, DM).T)
    wk_h = _tile_w(qkv3[:, 1].reshape(DM, DM).T)
    wv_h = _tile_w(qkv3[:, 2].reshape(DM, DM).T)
    wg_h = _tile_w(np.asarray(gate_w).T)
    wo_h = _tile_w(np.asarray(out_w).T)
    wf_h = _tile_w(np.asarray(ffn_w).T)
    biasr_h = np.ascontiguousarray(np.asarray(bias).reshape(NCH, 128).T)
    nwc_h = np.ascontiguousarray(np.asarray(norm_w).reshape(NCH, 128).T)
    ddt, qdt, kdc, bdr, psrs = host_constants()
    ident = np.eye(128, dtype=BFNP)
    in_maps = []
    for c in range(8):
        b, t0 = c // 4, (c % 4) * T
        xT_h = np.ascontiguousarray(np.asarray(x)[b, t0:t0 + T, :].T)
        in_maps.append({
            "xT": xT_h, "wq": wq_h, "wk": wk_h, "wv": wv_h, "wg": wg_h,
            "wo": wo_h, "wf": np.ascontiguousarray(wf_h[c:c + 1]),
            "biasr": biasr_h, "nwi": nwc_h,
            "ddi": ddt, "qdi": qdt, "kdi": kdc, "bdi": bdr,
            "psi": psrs[c], "idi": ident,
        })
    return in_maps


def kernel(L0_attn_hidden_states, L0_attn_qkv_w, L0_attn_out_w,
           L0_attn_gate_w, L0_attn_norm_w, L0_ffn_weight, L0_ffn_bias):
    global _NC_CACHE
    if _NC_CACHE is None:
        _NC_CACHE = build_program()
    nc = _NC_CACHE
    in_maps = host_inputs(L0_attn_hidden_states, L0_attn_qkv_w,
                          L0_attn_out_w, L0_attn_gate_w, L0_attn_norm_w,
                          L0_ffn_weight, L0_ffn_bias)
    res = run_bass_kernel_spmd(nc, in_maps, core_ids=list(range(8)))
    out = np.zeros((2, 4096, DM), np.float32)
    for c in range(8):
        b, t0 = c // 4, (c % 4) * T
        out[b, t0:t0 + T, :] = np.asarray(res.results[c]["yT"]).T
    return out


